# version 03
# baseline (speedup 1.0000x reference)
import os
from contextlib import ExitStack
import numpy as np
import ml_dtypes
import concourse.bass as bass
import concourse.mybir as mybir
from concourse.bass_utils import run_bass_kernel_spmd

F32 = mybir.dt.float32
BF16 = mybir.dt.bfloat16
AF = mybir.ActivationFunctionType
ALU = mybir.AluOpType
AX = mybir.AxisListType

B, T, E, NH, D = 4, 2048, 2048, 16, 128
NEXP, NXP = 32, 34
R = 1024
NEG = -1.0e30
ISQD = float(1.0 / np.sqrt(128.0))
EG = (8, 16)


def own_rows(p):
    if p == 0:
        return list(range(0, 512)) + list(range(1536, 2048))
    return list(range(512, 1536))


def host_prep(inp):
    f = np.ascontiguousarray
    x = np.asarray(inp["L0_attn_x"], dtype=np.float32)
    Wq, bq = np.asarray(inp["L0_attn_Wq"]), np.asarray(inp["L0_attn_bq"])
    Wk, bk = np.asarray(inp["L0_attn_Wk"]), np.asarray(inp["L0_attn_bk"])
    Wv, bv = np.asarray(inp["L0_attn_Wv"]), np.asarray(inp["L0_attn_bv"])
    Wo, bo = np.asarray(inp["L0_attn_Wo"]), np.asarray(inp["L0_attn_bo"])
    nw = np.asarray(inp["L0_ffn_norm_w"])
    rw = np.asarray(inp["L0_ffn_router_w"])
    w1 = np.asarray(inp["L0_ffn_w1"])
    w2 = np.asarray(inp["L0_ffn_w2"])
    sg = np.asarray(inp["L0_ffn_sg"])
    su = np.asarray(inp["L0_ffn_su"])
    sd = np.asarray(inp["L0_ffn_sd"])

    def head_stack(W):
        Wr = W.reshape(NH, 128, 16, 128)
        return f(Wr.transpose(0, 3, 2, 1).reshape(NH, 128, 2048))

    def f_tiles(W):
        Wr = W.reshape(4, 512, 16, 128)
        return f(Wr.transpose(0, 2, 3, 1))

    w1x = np.empty((NXP, 8, 128, 4096), dtype=np.float32)
    w1r = w1.reshape(NEXP, 16, 128, 2, 8, 128)
    w1x[:NEXP] = w1r.transpose(0, 4, 2, 1, 3, 5).reshape(NEXP, 8, 128, 4096)
    sgr = sg.reshape(16, 128, 16, 128)
    sur = su.reshape(16, 128, 16, 128)
    for hh in range(2):
        blk = np.stack([sgr[:, :, 8 * hh:8 * hh + 8, :],
                        sur[:, :, 8 * hh:8 * hh + 8, :]], axis=3)
        w1x[NEXP + hh] = blk.transpose(2, 1, 0, 3, 4).reshape(8, 128, 4096)
    w2x = np.empty((NXP, 8, 128, 2048), dtype=np.float32)
    w2x[:NEXP] = w2.reshape(NEXP, 8, 128, 2048)
    sdr = sd.reshape(16, 128, 2048)
    w2x[NEXP] = sdr[:8]
    w2x[NEXP + 1] = sdr[8:]

    j = np.arange(64, dtype=np.float32)
    thetas = np.float32(10000.0) ** (np.float32(-2.0) * j / np.float32(128.0))
    m_all = np.arange(1, T + 1, dtype=np.float32)
    ang = (m_all[None, :] * thetas[:, None]).astype(np.float32)
    cosf = np.concatenate([np.cos(ang), np.cos(ang)], 0).astype(np.float32)
    sinf = np.concatenate([np.sin(ang), np.sin(ang)], 0).astype(np.float32)

    shared = {
        "wq_h": head_stack(Wq), "wk_h": head_stack(Wk),
        "wv_t": f_tiles(Wv), "wo_t": f_tiles(Wo),
        "bq_d": f(bq.reshape(NH, 128).T), "bk_d": f(bk.reshape(NH, 128).T),
        "bv_d": f(bv.reshape(NH, 128).T), "bo_row": f(bo.reshape(1, E)),
        "nw_d": f(nw.reshape(16, 128).T),
        "rwt": f(rw.reshape(NEXP, 16, 128).transpose(1, 2, 0)),
        "w1x": w1x, "w2x": w2x, "ckc": cosf, "cks": sinf,
    }
    in_maps = []
    for c in range(8):
        b, p = c // 2, c % 2
        rows = own_rows(p)
        xb = x[b]
        m = dict(shared)
        m["xb_fm"] = f(xb.T)
        m["xo_fm"] = f(xb[rows].T)
        m["xo_tm"] = f(xb[rows])
        m["cqc"] = f(cosf[:, rows])
        m["cqs"] = f(sinf[:, rows])
        mask = np.zeros((16, 128, 512), dtype=np.float32)
        qpos = np.asarray(rows)
        for s in range(16):
            g = 0 if s < 8 else 1
            kglob = 128 * s + np.arange(128)
            qglob = qpos[512 * g: 512 * g + 512]
            mask[s] = np.where(kglob[:, None] <= qglob[None, :], 0.0, NEG)
        m["maskst"] = mask.astype(ml_dtypes.bfloat16)
        in_maps.append(m)
    return in_maps


gLGT, gRWT, gBRW1, gBRW2 = 14336, 15360, 10240, 12288


def build_program(taps=(), phase_stop=99):
    nc = bass.Bass("TRN2", target_bir_lowering=False)

    def din(name, shape, dtype=F32):
        return nc.dram_tensor(name, shape, dtype, kind="ExternalInput")

    xb_fm = din("xb_fm", [E, T]); xo_fm = din("xo_fm", [E, R]); xo_tm = din("xo_tm", [R, E])
    wq_h = din("wq_h", [NH, 128, 2048]); wk_h = din("wk_h", [NH, 128, 2048])
    wv_t = din("wv_t", [4, 16, 128, 512]); wo_t = din("wo_t", [4, 16, 128, 512])
    bq_d = din("bq_d", [128, 16]); bk_d = din("bk_d", [128, 16]); bv_d = din("bv_d", [128, 16])
    bo_row_d = din("bo_row", [1, E])
    nw_d = din("nw_d", [128, 16]); rwt_d = din("rwt", [16, 128, 32])
    w1x_d = din("w1x", [NXP, 8, 128, 4096]); w2x_d = din("w2x", [NXP, 8, 128, 2048])
    ckc_d = din("ckc", [128, T]); cks_d = din("cks", [128, T])
    cqc_d = din("cqc", [128, R]); cqs_d = din("cqs", [128, R])
    mask_d = din("maskst", [16, 128, 512], BF16)

    oo = nc.dram_tensor("oo", [8, 128, 2048], F32, kind="ExternalOutput")
    q_dram = nc.dram_tensor("q_dram", [NH, 128, R], BF16, kind="Internal")
    k_dram = nc.dram_tensor("k_dram", [NH, 128, T], BF16, kind="Internal")
    v_dram = nc.dram_tensor("v_dram", [4, 16, 128, 512], BF16, kind="Internal")
    s_dram = nc.dram_tensor("s_dram", [8, 128, 2048], F32, kind="Internal")

    tap_t = {}
    for tname, tshape, tdt in taps:
        tap_t[tname] = nc.dram_tensor("tap_" + tname, tshape, tdt, kind="ExternalOutput")

    ident_c = nc.inline_tensor(np.eye(128, dtype=np.float32), "identc")
    rotm = np.zeros((128, 128), dtype=np.float32)
    for m_ in range(64):
        rotm[m_ + 64, m_] = -1.0
        rotm[m_, m_ + 64] = 1.0
    rot_c = nc.inline_tensor(rotm, "rotc")

    al = nc.alloc_sbuf_tensor
    ones_f = al("ones_f", [128, 8], F32)
    ones_b = al("ones_b", [128, 8], BF16)
    onesr_f = al("onesr_f", [1, 128], F32)
    bqs = al("bqs", [128, 16], F32); bks = al("bks", [128, 16], F32); bvs = al("bvs", [128, 16], F32)
    nws = al("nws", [128, 16], F32)
    stats = al("stats", [128, 32], F32)
    gates = al("gates", [128, 8 * NXP], F32)
    rms_b = al("rms_b", [128, R], F32)
    xnw = al("xnw", [128, 16 * R], BF16)
    RD = al("RD", [128, 16384], BF16)
    RE = al("RE", [128, 12288], BF16)
    RF = al("RF", [128, 16384], BF16)
    G = al("G", [128, 16384], F32)
    H = al("H", [128, 4096], F32)
    rot_s = al("rot_s", [128, 128], F32)
    id_s = al("id_s", [128, 128], F32)
    cst = al("cst", [1, 4], F32)
    assert nc.sbuf_bytes_remaining >= 0, nc.sbuf_bytes_remaining

    ctx = ExitStack()
    sems = {}
    SEMNAMES = ["mmt", "vva", "ssa", "gga",
                "dl0", "dl1", "dg0", "dg1", "dgc", "dsl",
                "ds0", "ds1", "dm0", "dm1", "dw0", "dw1", "dst"]
    for sn in SEMNAMES:
        sems[sn] = ctx.enter_context(nc.semaphore(sn))
    cnt = {k: 0 for k in sems}

    TENr, VECr, SCAr, SYNr, GPSr = nc.tensor, nc.vector, nc.scalar, nc.sync, nc.gpsimd

    class Fenced:
        def __init__(self, eng, sn):
            object.__setattr__(self, "eng", eng)
            object.__setattr__(self, "sn", sn)
        def __getattr__(self, name):
            eng, sn = self.eng, self.sn
            fn = getattr(eng, name)
            def call(*a, **kw):
                eng.wait_ge(sems[sn], cnt[sn])
                ins = fn(*a, **kw)
                ins.then_inc(sems[sn], 1)
                cnt[sn] += 1
                return cnt[sn]
            return call
        def wait(self, sn2, val=None):
            self.eng.wait_ge(sems[sn2], cnt[sn2] if val is None else val)

    V = Fenced(VECr, "vva")
    S = Fenced(SCAr, "ssa")

    def tinc(ins):
        ins.then_inc(sems["mmt"], 1)
        cnt["mmt"] += 1
        return cnt["mmt"]

    def twait(sn, val=None):
        TENr.wait_ge(sems[sn], cnt[sn] if val is None else val)

    def dma(queue, sn, out, in_):
        ins = queue.dma_start(out=out, in_=in_)
        ins.then_inc(sems[sn], 16)
        cnt[sn] += 16
        return cnt[sn]

    def qwait(queue, sn, val=None):
        queue.wait_ge(sems[sn], cnt[sn] if val is None else val)

    def row(off, n):
        return G[0:1, off:off + n]

    def gcol(off, n):
        return G[:, off:off + n]

    def finish():
        qwait(SYNr, "dst"); qwait(SYNr, "ds0"); qwait(SYNr, "ds1")
        ctx.close()
        return nc

    for dst_, src_ in ((bqs[:], bq_d.ap()), (bks[:], bk_d.ap()), (bvs[:], bv_d.ap()),
                       (nws[:], nw_d.ap()), (id_s[:], ident_c.ap()), (rot_s[:], rot_c.ap())):
        dma(SYNr, "dsl", dst_, src_)
    dma(SYNr, "dsl", gcol(gRWT, 512).rearrange("p (k c) -> p k c", k=16),
        rwt_d.ap().rearrange("k p c -> p k c"))
    V.memset(ones_f[:], 1.0)
    V.memset(ones_b[:], 1.0)
    V.memset(onesr_f[:], 1.0)
    V.memset(cst[0:1, 0:1], 1.0e-5)
    V.memset(cst[0:1, 1:2], 1.0e-6)
    V.wait("dsl")
    for k in range(16):
        sl = gcol(gRWT + 32 * k, 32)
        V.tensor_scalar(sl, sl, nws[:, k:k + 1], None, ALU.mult)

    XB = (0, 1024); SQ = (2048, 3072)
    rMU, rVAR, rRSTD, rNEGM, rRMS, rTMP = 4096, 5120, 6144, 7168, 8192, 9216
    BC0, BC1 = 4096, 5120
    p1a = ExitStack()
    with nc.psum_tensor("ps_lgt", [32, 1024], F32) as ps_lgt:
        ps_sum = p1a.enter_context(nc.psum_tensor("ps_sum", [1, 1024], F32))
        ps_sq = p1a.enter_context(nc.psum_tensor("ps_sq", [1, 1024], F32))

        used1, sq1, ld1 = [], [], []
        for k in range(16):
            xb, sq = XB[k % 2], SQ[k % 2]
            par = "dl0" if k % 2 == 0 else "dl1"
            if k >= 2:
                qwait(SYNr, "mmt", used1[k - 2])
            ld1.append(dma(SYNr, par, gcol(xb, 1024), xo_fm[128 * k:128 * (k + 1), :]))
            S.wait(par, ld1[k])
            if k >= 2:
                S.wait("mmt", used1[k - 2])
            sq1.append(S.activation(gcol(sq, 1024), gcol(xb, 1024), AF.Square))
            twait("ssa", sq1[k])
            if k == 0:
                twait("vva")
            for h2 in range(2):
                TENr.matmul(ps_sum[:, 512 * h2:512 * (h2 + 1)], ones_f[:, 0:1],
                            gcol(xb + 512 * h2, 512), start=(k == 0), stop=(k == 15))
                ins = TENr.matmul(ps_sq[:, 512 * h2:512 * (h2 + 1)], ones_f[:, 0:1],
                                  gcol(sq + 512 * h2, 512), start=(k == 0), stop=(k == 15))
            used1.append(tinc(ins))

        V.wait("mmt")
        V.tensor_scalar(row(rMU, 1024), ps_sum[0:1, :], 1.0 / 2048.0, None, ALU.mult)
        V.tensor_scalar(row(rVAR, 1024), ps_sq[0:1, :], 1.0 / 2048.0, None, ALU.mult)
        V.tensor_tensor(row(rTMP, 1024), row(rMU, 1024), row(rMU, 1024), ALU.mult)
        V.tensor_tensor(row(rVAR, 1024), row(rVAR, 1024), row(rTMP, 1024), ALU.subtract)
        S.wait("vva")
        S.activation(row(rTMP, 1024), row(rVAR, 1024), AF.Ln, bias=cst[0:1, 0:1])
        S.activation(row(rRSTD, 1024), row(rTMP, 1024), AF.Exp, scale=-0.5)
        V.wait("ssa")
        V.scalar_tensor_tensor(row(rNEGM, 1024), row(rMU, 1024), -1.0, row(rRSTD, 1024), ALU.mult, ALU.mult)
        V.tensor_scalar(row(rTMP, 1024), row(rVAR, 1024), 1.0e-5, None, ALU.add)
        V.reciprocal(row(rMU, 1024), row(rTMP, 1024))
        V.tensor_tensor(row(rVAR, 1024), row(rVAR, 1024), row(rMU, 1024), ALU.mult)
        S.wait("vva")
        S.activation(row(rTMP, 1024), row(rVAR, 1024), AF.Ln, bias=cst[0:1, 1:2])
        S.activation(row(rRMS, 1024), row(rTMP, 1024), AF.Exp, scale=-0.5)
        p1a.close()
        ps_bc = p1a.enter_context(nc.psum_tensor("ps_bc", [128, 1024], F32))
        ps_st = p1a.enter_context(nc.psum_tensor("ps_st", [128, 32], F32))

        twait("ssa")
        for i, roff in enumerate((rRSTD, rNEGM, rRMS)):
            for tc in range(8):
                ins = TENr.matmul(ps_st[:, 8 * i + tc:8 * i + tc + 1], row(roff + 128 * tc, 128),
                                  onesr_f[0:1, 0:1], start=True, stop=True)
        m_st = tinc(ins)
        V.wait("mmt", m_st)
        V.tensor_copy(stats[:, 0:24], ps_st[:, 0:24])

        for src_, dst_ in ((rRMS, rms_b[:, :]), (rRSTD, gcol(BC0, 1024)), (rNEGM, gcol(BC1, 1024))):
            twait("vva")
            for h2 in range(2):
                ins = TENr.matmul(ps_bc[:, 512 * h2:512 * (h2 + 1)], onesr_f[:, :],
                                  row(src_ + 512 * h2, 512), start=True, stop=True)
            m_ = tinc(ins)
            V.wait("mmt", m_)
            V.tensor_copy(dst_, ps_bc[:, :])

        used2, ld2, xn2, vv2 = [], [], [], []
        qwait(SYNr, "ssa"); qwait(SYNr, "mmt"); qwait(SYNr, "vva")
        for k in range(16):
            xb, sc = XB[k % 2], SQ[k % 2]
            par = "dl0" if k % 2 == 0 else "dl1"
            if k >= 2:
                qwait(SYNr, "vva", vv2[k - 2])
            ld2.append(dma(SYNr, par, gcol(xb, 1024), xo_fm[128 * k:128 * (k + 1), :]))
            V.wait(par, ld2[k])
            if k >= 2:
                V.wait("mmt", used2[k - 2])
                V.wait("ssa", xn2[k - 2])
            c_tt = V.tensor_tensor(gcol(sc, 1024), gcol(xb, 1024), gcol(BC0, 1024), ALU.mult)
            vv2.append(c_tt)
            V.tensor_tensor(gcol(sc, 1024), gcol(sc, 1024), gcol(BC1, 1024), ALU.add)
            c_xnw = V.tensor_scalar(xnw[:, R * k:R * (k + 1)], gcol(sc, 1024), nws[:, k:k + 1], None, ALU.mult)
            twait("vva", c_xnw)
            TENr.matmul(ps_lgt[:, 0:512], gcol(gRWT + 32 * k, 32), gcol(sc, 512), start=(k == 0), stop=(k == 15))
            ins = TENr.matmul(ps_lgt[:, 512:1024], gcol(gRWT + 32 * k, 32), gcol(sc + 512, 512), start=(k == 0), stop=(k == 15))
            used2.append(tinc(ins))
            S.wait("vva", c_xnw)
            xn2.append(S.copy(RD[:, R * k:R * (k + 1)], gcol(sc, 1024)))

        V.wait("mmt")
        V.tensor_copy(G[0:32, gLGT:gLGT + 1024], ps_lgt[:, :])
        p1a.close()

    def tap(name, src_ap):
        if name in tap_t:
            for sn_ in ("vva", "ssa", "mmt", "gga", "ds0", "ds1", "dgc", "dst"):
                qwait(SYNr, sn_)
            dma(SYNr, "dst", tap_t[name].ap(), src_ap)

    tap("xnw", xnw[:, :])
    tap("xn_own", RD[:, 0:16 * R])
    tap("lgt", G[0:32, gLGT:gLGT + 1024])
    tap("stats", stats[:, 0:24])
    tap("rmsb", rms_b[:, :])
    if phase_stop <= 1:
        return finish()

    bMU, bVAR, bTMP = 0, 2048, 4096
    with nc.psum_tensor("ps_sum2", [1, 2048], F32) as ps_sum2, \
         nc.psum_tensor("ps_sq2", [1, 2048], F32) as ps_sq2:
        usedb, sqb, ldb = [], [], []
        for k in range(16):
            xb, sq = 2048 * (k % 2), 4096 + 2048 * (k % 2)
            par = "dg0" if k % 2 == 0 else "dg1"
            if k >= 2:
                qwait(GPSr, "mmt", usedb[k - 2])
            ldb.append(dma(GPSr, par, RE[:, xb:xb + 2048], xb_fm[128 * k:128 * (k + 1), :]))
            S.wait(par, ldb[k])
            if k >= 2:
                S.wait("mmt", usedb[k - 2])
            sqb.append(S.activation(RE[:, sq:sq + 2048], RE[:, xb:xb + 2048], AF.Square))
            twait("ssa", sqb[k])
            if k == 0:
                twait("vva")
            for h4 in range(4):
                TENr.matmul(ps_sum2[:, 512 * h4:512 * (h4 + 1)], ones_b[:, 0:1],
                            RE[:, xb + 512 * h4:xb + 512 * (h4 + 1)], start=(k == 0), stop=(k == 15))
                ins = TENr.matmul(ps_sq2[:, 512 * h4:512 * (h4 + 1)], ones_b[:, 0:1],
                                  RE[:, sq + 512 * h4:sq + 512 * (h4 + 1)], start=(k == 0), stop=(k == 15))
            usedb.append(tinc(ins))

        V.wait("mmt")
        V.tensor_scalar(row(bMU, 2048), ps_sum2[0:1, :], 1.0 / 2048.0, None, ALU.mult)
        V.tensor_scalar(row(bVAR, 2048), ps_sq2[0:1, :], 1.0 / 2048.0, None, ALU.mult)
        V.tensor_tensor(row(bTMP, 2048), row(bMU, 2048), row(bMU, 2048), ALU.mult)
        V.tensor_tensor(row(bVAR, 2048), row(bVAR, 2048), row(bTMP, 2048), ALU.subtract)
        S.wait("vva")
        S.activation(row(bTMP, 2048), row(bVAR, 2048), AF.Ln, bias=cst[0:1, 0:1])
        S.activation(row(gBRW1, 2048), row(bTMP, 2048), AF.Exp, scale=-0.5)
        V.wait("ssa")
        V.scalar_tensor_tensor(row(gBRW2, 2048), row(bMU, 2048), -1.0, row(gBRW1, 2048), ALU.mult, ALU.mult)

    tap("brw1", row(gBRW1, 2048))
    tap("brw2", row(gBRW2, 2048))
    if phase_stop <= 2:
        return finish()

    lgS, swS, scS, gnS = 0, 32, 64, 128
    mxS, m4S, dnS = 96, 97, 98
    with nc.psum_tensor("ps_t7", [128, 32], F32) as ps_t7:
        for tc in range(8):
            twait("vva")
            ins = TENr.transpose(ps_t7[:, 0:32], G[0:32, gLGT + 128 * tc:gLGT + 128 * (tc + 1)], id_s[0:32, 0:32])
            m_ = tinc(ins)
            V.wait("mmt", m_)
            V.tensor_copy(gcol(lgS, 32), ps_t7[:, 0:32])
            V.tensor_scalar(gcol(lgS, 32), gcol(lgS, 32), stats[:, 16 + tc:17 + tc], None, ALU.mult)
            V.reduce_max(gcol(mxS, 1), gcol(lgS, 32), axis=AX.X, negate=True)
            S.wait("vva")
            S.activation(gcol(scS, 32), gcol(lgS, 32), AF.Exp, bias=gcol(mxS, 1))
            V.wait("ssa")
            V.tensor_copy(gcol(swS, 32), gcol(scS, 32))
            for r_ in range(3):
                V.reduce_max(gcol(m4S, 1), gcol(swS, 32), axis=AX.X)
                V.tensor_scalar(gcol(gnS, 32), gcol(swS, 32), gcol(m4S, 1), None, ALU.is_ge)
                V.scalar_tensor_tensor(gcol(swS, 32), gcol(gnS, 32), NEG, gcol(swS, 32), ALU.mult, ALU.add)
            V.reduce_max(gcol(m4S, 1), gcol(swS, 32), axis=AX.X)
            V.tensor_scalar(gcol(gnS, 32), gcol(scS, 32), gcol(m4S, 1), None, ALU.is_ge)
            V.tensor_tensor(gcol(gnS, 32), gcol(scS, 32), gcol(gnS, 32), ALU.mult)
            V.reduce_sum(gcol(dnS, 1), gcol(gnS, 32), axis=AX.X)
            V.reciprocal(gcol(m4S, 1), gcol(dnS, 1))
            V.tensor_scalar(gcol(gnS, 32), gcol(gnS, 32), gcol(m4S, 1), None, ALU.mult)
            V.tensor_scalar(gates[:, NXP * tc:NXP * tc + 32], gcol(gnS, 32), stats[:, 16 + tc:17 + tc], None, ALU.mult)
            V.tensor_copy(gates[:, NXP * tc + 32:NXP * tc + 33], stats[:, 16 + tc:17 + tc])
            V.tensor_copy(gates[:, NXP * tc + 33:NXP * tc + 34], stats[:, 16 + tc:17 + tc])

    tap("gates", gates[:, :])
    if phase_stop <= 3:
        return finish()

    cqcS, cqsS, qfS, t1S, t2S = 0, 1024, 2048, 4096, 5120
    qwait(SYNr, "vva"); qwait(SYNr, "ssa")
    dma(SYNr, "dsl", gcol(cqcS, 1024), cqc_d.ap())
    dma(SYNr, "dsl", gcol(cqsS, 1024), cqs_d.ap())
    with nc.psum_tensor("ps_q", [128, 2 * 1024], F32) as ps_q, \
         nc.psum_tensor("ps_r", [128, 2 * 1024], F32) as ps_r:
        qmm, qev_bias, qev_rd, qev_q, qst = [], [], [], [], []
        qwait(GPSr, "mmt"); qwait(GPSr, "ssa"); qwait(GPSr, "vva")

        def q_stageA(h):
            par = "dg0" if h % 2 == 0 else "dg1"
            ofs = 2048 * (h % 2)
            if h >= 2:
                qwait(GPSr, "mmt", qmm[h - 2])
            wql = dma(GPSr, par, RE[:, ofs:ofs + 2048], wq_h[h])
            twait(par, wql)
            if h == 0:
                twait("ssa"); twait("dsl"); twait("vva")
            if h >= 2:
                twait("vva", qev_bias[h - 2])
            for k in range(16):
                for h2 in range(2):
                    ins = TENr.matmul(ps_q[:, 1024 * (h % 2) + 512 * h2:1024 * (h % 2) + 512 * (h2 + 1)],
                                      RE[:, ofs + 128 * k:ofs + 128 * (k + 1)],
                                      RD[:, R * k + 512 * h2:R * k + 512 * (h2 + 1)],
                                      start=(k == 0), stop=(k == 15))
            qmm.append(tinc(ins))
            V.wait("mmt", qmm[h])
            if h >= 2:
                V.wait("mmt", qrot_mm[h - 2] if len(qrot_mm) > h - 2 else 0)
            qev_bias.append(V.tensor_scalar(gcol(qfS + 1024 * (h % 2), 1024),
                                            ps_q[:, 1024 * (h % 2):1024 * (h % 2) + 1024],
                                            bqs[:, h:h + 1], None, ALU.add))

        qrot_mm = []

        def q_stageB(h):
            twait("vva", qev_bias[h])
            if h >= 2:
                twait("vva", qev_rd[h - 2])
            for h2 in range(2):
                ins = TENr.matmul(ps_r[:, 1024 * (h % 2) + 512 * h2:1024 * (h % 2) + 512 * (h2 + 1)],
                                  rot_s[:, :], gcol(qfS + 1024 * (h % 2) + 512 * h2, 512), start=True, stop=True)
            qrot_mm.append(tinc(ins))
            V.wait("mmt", qrot_mm[h])
            V.tensor_tensor(gcol(t1S, 1024), gcol(qfS + 1024 * (h % 2), 1024), gcol(cqcS, 1024), ALU.mult)
            qev_rd.append(V.tensor_tensor(gcol(t2S, 1024), ps_r[:, 1024 * (h % 2):1024 * (h % 2) + 1024],
                                          gcol(cqsS, 1024), ALU.mult))
            spar = "ds0" if h % 2 == 0 else "ds1"
            if h >= 2:
                V.wait(spar, qst[h - 2])
            qev_q.append(V.tensor_tensor(RE[:, 4096 + 1024 * (h % 2):4096 + 1024 * (h % 2) + 1024],
                                         gcol(t1S, 1024), gcol(t2S, 1024), ALU.add))
            qwait(SYNr, "vva", qev_q[h])
            qst.append(dma(SYNr, spar, q_dram[h], RE[:, 4096 + 1024 * (h % 2):4096 + 1024 * (h % 2) + 1024]))

        for h in range(NH):
            q_stageA(h)
            if h >= 1:
                q_stageB(h - 1)
        q_stageB(NH - 1)
    tap("q_dram", q_dram.ap())
    if phase_stop <= 4:
        return finish()

    ckcS, cksS, kfS, t1S4, BCh0, BCh1, t2S4 = 0, 2048, 4096, 6144, 6144, 7168, 8192
    qwait(SYNr, "vva")
    dma(SYNr, "dsl", gcol(ckcS, 2048), ckc_d.ap())
    dma(SYNr, "dsl", gcol(cksS, 2048), cks_d.ap())
    with nc.psum_tensor("ps_k", [128, 2 * 1024], F32) as ps_k, \
         nc.psum_tensor("ps_kr", [128, 1024], F32) as ps_kr, \
         nc.psum_tensor("ps_v", [128, 2 * 512], F32) as ps_v:
        ps_bch = ps_v
        for hh in range(2):
            chs = slice(1024 * hh, 1024 * (hh + 1))
            twait("vva"); twait("ssa")
            for h2 in range(2):
                ins = TENr.matmul(ps_bch[:, 512 * h2:512 * (h2 + 1)], onesr_f[:, :],
                                  row(gBRW1 + 1024 * hh + 512 * h2, 512), start=True, stop=True)
            m_ = tinc(ins)
            V.wait("mmt", m_)
            V.tensor_copy(gcol(BCh0, 1024), ps_bch[:, :])
            twait("vva")
            for h2 in range(2):
                ins = TENr.matmul(ps_bch[:, 512 * h2:512 * (h2 + 1)], onesr_f[:, :],
                                  row(gBRW2 + 1024 * hh + 512 * h2, 512), start=True, stop=True)
            m_ = tinc(ins)
            V.wait("mmt", m_)
            V.tensor_copy(gcol(BCh1, 1024), ps_bch[:, :])
            xnb, xbld = [], []
            qwait(GPSr, "vva"); qwait(GPSr, "ssa"); qwait(GPSr, "mmt")
            V.wait("mmt")
            for k in range(16):
                par = "dg0" if k % 2 == 0 else "dg1"
                ofs = 2048 * (k % 2)
                if k >= 2:
                    qwait(GPSr, "vva", xnb[k - 2])
                xbld.append(dma(GPSr, par, RE[:, ofs:ofs + 1024], xb_fm[128 * k:128 * (k + 1), chs]))
                V.wait(par, xbld[k])
                V.tensor_tensor(gcol(t2S4, 1024), RE[:, ofs:ofs + 1024], gcol(BCh0, 1024), ALU.mult)
                xnb.append(V.tensor_tensor(RD[:, 1024 * k:1024 * (k + 1)], gcol(t2S4, 1024), gcol(BCh1, 1024), ALU.add))
            kmm, kev_bias, kev_rd, kev_k, kst, krot_mm = [], [], [], [], [], []
            V.wait("ds0"); V.wait("ds1")
            qwait(GPSr, "vva")

            def k_stageA(h):
                par = "dg0" if h % 2 == 0 else "dg1"
                ofs = 2048 * (h % 2)
                if h >= 2:
                    qwait(GPSr, "mmt", kmm[h - 2])
                wkl = dma(GPSr, par, RE[:, ofs:ofs + 2048], wk_h[h])
                twait(par, wkl)
                if h == 0:
                    twait("vva"); twait("dsl")
                if h >= 2:
                    twait("vva", kev_bias[h - 2])
                for k in range(16):
                    for h2 in range(2):
                        ins = TENr.matmul(ps_k[:, 1024 * (h % 2) + 512 * h2:1024 * (h % 2) + 512 * (h2 + 1)],
                                          RE[:, ofs + 128 * k:ofs + 128 * (k + 1)],
                                          RD[:, 1024 * k + 512 * h2:1024 * k + 512 * (h2 + 1)],
                                          start=(k == 0), stop=(k == 15))
                kmm.append(tinc(ins))
                V.wait("mmt", kmm[h])
                if h >= 2:
                    V.wait("mmt", krot_mm[h - 2])
                kev_bias.append(V.tensor_scalar(gcol(kfS + 1024 * (h % 2), 1024),
                                                ps_k[:, 1024 * (h % 2):1024 * (h % 2) + 1024],
                                                bks[:, h:h + 1], None, ALU.add))

            def k_stageB(h):
                twait("vva", kev_bias[h])
                if h >= 1:
                    twait("vva", kev_rd[h - 1])
                for h2 in range(2):
                    ins = TENr.matmul(ps_kr[:, 512 * h2:512 * (h2 + 1)],
                                      rot_s[:, :], gcol(kfS + 1024 * (h % 2) + 512 * h2, 512), start=True, stop=True)
                krot_mm.append(tinc(ins))
                V.wait("mmt", krot_mm[h])
                V.tensor_tensor(gcol(t1S4, 1024), gcol(kfS + 1024 * (h % 2), 1024), gcol(ckcS + 1024 * hh, 1024), ALU.mult)
                kev_rd.append(V.tensor_tensor(gcol(t2S4, 1024), ps_kr[:, 0:1024],
                                              gcol(cksS + 1024 * hh, 1024), ALU.mult))
                spar = "ds0" if h % 2 == 0 else "ds1"
                if h >= 2:
                    V.wait(spar, kst[h - 2])
                kev_k.append(V.tensor_tensor(RE[:, 4096 + 1024 * (h % 2):4096 + 1024 * (h % 2) + 1024],
                                             gcol(t1S4, 1024), gcol(t2S4, 1024), ALU.add))
                qwait(SYNr, "vva", kev_k[h])
                kst.append(dma(SYNr, spar, k_dram[h][:, chs],
                               RE[:, 4096 + 1024 * (h % 2):4096 + 1024 * (h % 2) + 1024]))

            for h in range(NH):
                k_stageA(h)
                if h >= 1:
                    k_stageB(h - 1)
            k_stageB(NH - 1)
            vst = []
            vmm = []
            for ft in range(4):
                if ft >= 1:
                    qwait(GPSr, "mmt", vmm[-1])
                if ft == 0 and hh == 1:
                    qwait(GPSr, "mmt")
                wvl = dma(GPSr, "dgc", RF[:, 0:8192].rearrange("p (k c) -> p k c", k=16),
                          wv_t[ft].rearrange("k p c -> p k c"))
                twait("dgc", wvl)
                if ft == 0:
                    twait("vva")
                for tk in range(8):
                    pv = tk % 2
                    nn = len(vst)
                    if nn >= 2:
                        twait("vva", vst[nn - 2][0])
                    for k in range(16):
                        ins = TENr.matmul(ps_v[:, 512 * pv:512 * (pv + 1)],
                                          RD[:, 1024 * k + 128 * tk:1024 * k + 128 * (tk + 1)],
                                          RF[:, 512 * k:512 * (k + 1)],
                                          start=(k == 0), stop=(k == 15))
                    m_v = tinc(ins)
                    V.wait("mmt", m_v)
                    spar = "ds0" if nn % 2 == 0 else "ds1"
                    if nn >= 2:
                        V.wait(spar, vst[nn - 2][1])
                    c_v = V.tensor_copy(RE[:, 8192 + 512 * pv:8192 + 512 * (pv + 1)], ps_v[:, 512 * pv:512 * (pv + 1)])
                    qwait(SYNr, "vva", c_v)
                    st_ = dma(SYNr, spar, v_dram[ft, 8 * hh + tk], RE[:, 8192 + 512 * pv:8192 + 512 * (pv + 1)])
                    vst.append((c_v, st_))
                vmm.append(cnt["mmt"])
    tap("k_dram", k_dram.ap())
    tap("v_dram", v_dram.ap())
    if phase_stop <= 5:
        return finish()

    PTo, KHo, QHo, MKo = 0, 8192, 12288, 14336
    linvS, lbS, yscrS = 0, 1024, 2048
    with nc.psum_tensor("ps_s", [128, 2 * 1024], F32) as ps_s, \
         nc.psum_tensor("ps_l2", [128, 1024], F32) as ps_l2, \
         nc.psum_tensor("ps_y", [128, 1024], F32) as ps_y:
        qwait(SYNr, "vva"); qwait(SYNr, "ssa"); qwait(SYNr, "mmt")
        qwait(SYNr, "ds0"); qwait(SYNr, "ds1")
        qwait(GPSr, "ds0"); qwait(GPSr, "ds1")
        khld, qhld, st_done, yevs, lrec, lbcp = [], [], [], [], [], []
        exps_pairs = []
        vgld = [None] * 4
        mskld, mskcons = [], []
        for h in range(NH):
            par = "dl0" if h % 2 == 0 else "dl1"
            kofs = KHo + 2048 * (h % 2)
            qofs = QHo + 1024 * (h % 2)
            if h >= 2:
                qwait(SYNr, "mmt", st_done[h - 2])
            khld.append(dma(SYNr, par, RD[:, kofs:kofs + 2048], k_dram[h]))
            qhld.append(dma(SYNr, par, RD[:, qofs:qofs + 1024], q_dram[h]))
            if h % 4 == 0:
                qwait(GPSr, "mmt")
                for tkv in range(16):
                    vg_m = dma(GPSr, "dgc", RE[:, 512 * tkv:512 * (tkv + 1)], v_dram[h // 4, tkv])
                vgld[h // 4] = vg_m
            pv_done = None
            for g01 in range(2):
                Eg = EG[g01]
                exp_marks = []
                for pr in range(Eg // 2):
                    jt0 = 2 * pr
                    slot = len(exps_pairs) % 2
                    twait(par, qhld[h])
                    if pr == 0 and g01 == 0 and h == 0:
                        twait("vva"); twait("ssa")
                    if pr == 0 and g01 == 0 and h % 4 == 0:
                        twait("dgc", vgld[h // 4])
                    if len(exps_pairs) >= 2:
                        twait("ssa", exps_pairs[-2])
                    for dj in range(2):
                        ins = TENr.matmul(ps_s[:, 1024 * slot + 512 * dj:1024 * slot + 512 * (dj + 1)],
                                          RD[:, kofs + 128 * (jt0 + dj):kofs + 128 * (jt0 + dj + 1)],
                                          RD[:, qofs + 512 * g01:qofs + 512 * (g01 + 1)],
                                          start=True, stop=True)
                    m_s = tinc(ins)
                    masked = (g01 == 0) or (jt0 >= 8)
                    if g01 == 1 and pr == 0:
                        S.wait("mmt", pv_done)
                    if g01 == 0 and pr == 0 and h >= 1:
                        S.wait("mmt", st_done[h - 1])
                    if masked:
                        mpar = "dm0" if len(mskld) % 2 == 0 else "dm1"
                        if len(mskld) >= 2:
                            qwait(SYNr, "vva", mskcons[len(mskld) - 2])
                        mof = MKo + 1024 * (len(mskld) % 2)
                        mskld.append(dma(SYNr, mpar,
                                         RD[:, mof:mof + 1024].rearrange("p (s c) -> p s c", s=2),
                                         mask_d.ap()[jt0:jt0 + 2].rearrange("s p c -> p s c")))
                        V.wait(mpar, mskld[-1])
                        V.wait("mmt", m_s)
                        c_m = V.tensor_tensor(ps_s[:, 1024 * slot:1024 * (slot + 1)],
                                              ps_s[:, 1024 * slot:1024 * (slot + 1)],
                                              RD[:, mof:mof + 1024], ALU.add)
                        mskcons.append(c_m)
                        S.wait("vva", c_m)
                    else:
                        S.wait("mmt", m_s)
                    em = S.activation(RD[:, PTo + 512 * jt0:PTo + 512 * (jt0 + 2)],
                                      ps_s[:, 1024 * slot:1024 * (slot + 1)], AF.Exp, scale=ISQD)
                    exp_marks.append(em)
                    exps_pairs.append(em)
                if g01 == 0:
                    if h >= 1:
                        twait("vva", lrec[h - 1])
                        twait("vva", yevs[h - 1])
                        twait("vva", lbcp[h - 1])
                for jt in range(Eg):
                    twait("ssa", exp_marks[jt // 2])
                    TENr.matmul(ps_l2[0:1, 512 * g01:512 * (g01 + 1)], ones_b[:, 0:1],
                                RD[:, PTo + 512 * jt:PTo + 512 * (jt + 1)],
                                start=(jt == 0), stop=(jt == Eg - 1))
                    ins = TENr.matmul(ps_y[:, 512 * g01:512 * (g01 + 1)],
                                      RE[:, 512 * jt + 128 * (h % 4):512 * jt + 128 * (h % 4) + 128],
                                      RD[:, PTo + 512 * jt:PTo + 512 * (jt + 1)],
                                      start=(jt == 0), stop=(jt == Eg - 1))
                pv_done = tinc(ins)
            st_done.append(pv_done)
            V.wait("mmt", st_done[h])
            c_li = V.reciprocal(row(linvS, 1024), ps_l2[0:1, :])
            lrec.append(c_li)
            twait("vva", c_li)
            for h2 in range(2):
                ins = TENr.matmul(ps_l2[:, 512 * h2:512 * (h2 + 1)], onesr_f[:, :],
                                  row(linvS + 512 * h2, 512), start=True, stop=True)
            m_lb = tinc(ins)
            V.wait("mmt", m_lb)
            lbcp.append(V.tensor_copy(gcol(lbS, 1024), ps_l2[:, :]))
            V.tensor_tensor(gcol(yscrS, 1024), ps_y[:, :], gcol(lbS, 1024), ALU.mult)
            yevs.append(V.tensor_scalar(RF[:, 1024 * h:1024 * (h + 1)], gcol(yscrS, 1024),
                                        bvs[:, h:h + 1], None, ALU.add))
    tap("y_fm", RF[:, 0:16384])
    if phase_stop <= 6:
        return finish()

    boS, borowS = 0, 2048
    with nc.psum_tensor("ps_o", [128, 2 * 512], F32) as ps_o, \
         nc.psum_tensor("ps_bo", [128, 2048], F32) as ps_bo:
        qwait(SYNr, "vva")
        dma(SYNr, "dsl", row(borowS, 2048), bo_row_d.ap())
        twait("dsl"); twait("vva")
        for q4 in range(4):
            ins = TENr.matmul(ps_bo[:, 512 * q4:512 * (q4 + 1)], onesr_f[:, :],
                              row(borowS + 512 * q4, 512), start=True, stop=True)
        m_bo = tinc(ins)
        V.wait("mmt", m_bo)
        V.tensor_copy(gcol(boS, 2048), ps_bo[:, :])
        ost = []
        womm = []
        for ft in range(4):
            if ft >= 1:
                qwait(GPSr, "mmt", womm[ft - 1])
            if ft == 0:
                qwait(GPSr, "mmt"); qwait(GPSr, "vva")
            wol = dma(GPSr, "dgc", RE[:, 0:8192].rearrange("p (k c) -> p k c", k=16),
                      wo_t[ft].rearrange("k p c -> p k c"))
            twait("dgc", wol)
            for tc in range(8):
                pv = tc % 2
                nn = len(ost)
                if nn >= 2:
                    twait("vva", ost[nn - 2][0])
                for k in range(16):
                    ins = TENr.matmul(ps_o[:, 512 * pv:512 * (pv + 1)],
                                      RF[:, 1024 * k + 128 * tc:1024 * k + 128 * (tc + 1)],
                                      RE[:, 512 * k:512 * (k + 1)],
                                      start=(k == 0), stop=(k == 15))
                m_o = tinc(ins)
                V.wait("mmt", m_o)
                spar = "ds0" if nn % 2 == 0 else "ds1"
                if nn >= 2:
                    V.wait(spar, ost[nn - 2][1])
                c_o = V.tensor_tensor(H[:, 512 * pv:512 * (pv + 1)], ps_o[:, 512 * pv:512 * (pv + 1)],
                                      gcol(boS + 512 * ft, 512), ALU.add)
                qwait(SYNr, "vva", c_o)
                st_ = dma(SYNr, spar, s_dram[tc][:, 512 * ft:512 * (ft + 1)], H[:, 512 * pv:512 * (pv + 1)])
                ost.append((c_o, st_))
            womm.append(cnt["mmt"])
    tap("s_dram", s_dram.ap())
    if phase_stop <= 7:
        return finish()

    with nc.psum_tensor("ps_g", [128, 2 * 512], F32) as ps_g, \
         nc.psum_tensor("ps_u", [128, 2 * 512], F32) as ps_u, \
         nc.psum_tensor("ps_w2", [128, 2 * 512], F32) as ps_w2:
        V.wait("mmt"); V.wait("ssa"); V.wait("ds0"); V.wait("ds1"); V.wait("dst")
        V.memset(G[:, 0:8192], 0.0)
        V.memset(G[:, 8192:16384], 0.0)
        w1mm = {}
        w1ld = []
        evac = {}
        w2mm = [0] * NXP
        w2fhmm = {}
        sttd = []
        qwait(GPSr, "vva"); qwait(GPSr, "ssa"); qwait(GPSr, "mmt")
        for e in range(NXP):
            for jj in range(8):
                gi = 8 * e + jj
                par = "dg0" if gi % 2 == 0 else "dg1"
                ofs = 8192 + 4096 * (gi % 2)
                if gi >= 2:
                    qwait(GPSr, "mmt", w1mm[(gi - 2, 1)])
                w1ld.append(dma(GPSr, par, RD[:, ofs:ofs + 4096], w1x_d[e, jj]))
                twait(par, w1ld[gi])
                for th in range(2):
                    if gi == 0 and th == 0:
                        twait("vva"); twait("ssa")
                    if gi >= 1:
                        twait("vva", evac[(gi - 1, th)])
                    if jj == 0 and th == 0 and e >= 1:
                        twait("mmt", w2mm[e - 1])
                    for half, ps in ((0, ps_g), (1, ps_u)):
                        for k in range(16):
                            ins = TENr.matmul(ps[:, 512 * th:512 * (th + 1)],
                                              RD[:, ofs + 256 * k + 128 * half:ofs + 256 * k + 128 * (half + 1)],
                                              xnw[:, R * k + 512 * th:R * k + 512 * (th + 1)],
                                              start=(k == 0), stop=(k == 15))
                    w1mm[(gi, th)] = tinc(ins)
                    V.wait("mmt", w1mm[(gi, th)])
                    V.tensor_tensor(H[:, 0:512], ps_g[:, 512 * th:512 * (th + 1)],
                                    rms_b[:, 512 * th:512 * (th + 1)], ALU.mult)
                    S.wait("vva")
                    S.activation(H[:, 512:1024], H[:, 0:512], AF.Silu)
                    V.wait("ssa")
                    evac[(gi, th)] = V.tensor_tensor(RD[:, 1024 * jj + 512 * th:1024 * jj + 512 * (th + 1)],
                                                     H[:, 512:1024], ps_u[:, 512 * th:512 * (th + 1)], ALU.mult)
            for fh in range(2):
                gi2 = 2 * e + fh
                ofs2 = 8192 * (gi2 % 2)
                wpar = "dw0" if gi2 % 2 == 0 else "dw1"
                if e >= 1:
                    qwait(GPSr, "mmt", w2fhmm[(e - 1, fh)])
                w2l = dma(GPSr, wpar, RF[:, ofs2:ofs2 + 8192].rearrange("p (i c) -> p i c", i=8),
                          w2x_d[e][:, :, 1024 * fh:1024 * (fh + 1)].rearrange("i p c -> p i c"))
                twait(wpar, w2l)
                twait("vva", evac[(8 * e + 7, 1)])
                for tc in range(8):
                    for f2 in range(2):
                        sidx = len(sttd)
                        pv = sidx % 2
                        if sidx >= 2:
                            twait("vva", sttd[sidx - 2])
                        for i in range(8):
                            ins = TENr.matmul(ps_w2[:, 512 * pv:512 * (pv + 1)],
                                              RD[:, 1024 * i + 128 * tc:1024 * i + 128 * (tc + 1)],
                                              RF[:, ofs2 + 1024 * i + 512 * f2:ofs2 + 1024 * i + 512 * (f2 + 1)],
                                              start=(i == 0), stop=(i == 7))
                        m_w2 = tinc(ins)
                        V.wait("mmt", m_w2)
                        go = 2048 * tc + 1024 * fh + 512 * f2
                        sttd.append(V.scalar_tensor_tensor(gcol(go, 512), ps_w2[:, 512 * pv:512 * (pv + 1)],
                                                           gates[:, NXP * tc + e:NXP * tc + e + 1],
                                                           gcol(go, 512), ALU.mult, ALU.add))
                w2fhmm[(e, fh)] = cnt["mmt"]
            w2mm[e] = cnt["mmt"]
    tap("routed", G[:, :])
    if phase_stop <= 8:
        return finish()

    xld, sld = [], []
    outst = []
    qwait(SYNr, "vva"); qwait(SYNr, "ssa"); qwait(SYNr, "mmt")
    for it in range(16):
        tc, hf = it // 2, it % 2
        tpo = 2048 + 1024 * (it % 2)
        if it >= 1:
            qwait(SYNr, "vva")
        xld.append(dma(SYNr, "dl0", H[:, 0:1024], xo_tm[128 * tc:128 * (tc + 1), 1024 * hf:1024 * (hf + 1)]))
        sld.append(dma(SYNr, "dl1", H[:, 1024:2048], s_dram[tc][:, 1024 * hf:1024 * (hf + 1)]))
        V.wait("dl0", xld[-1]); V.wait("dl1", sld[-1])
        if it >= 2:
            V.wait("ds0" if it % 2 == 0 else "ds1", outst[it - 2])
        V.tensor_scalar(H[:, tpo:tpo + 1024], H[:, 0:1024], stats[:, tc:tc + 1], stats[:, 8 + tc:9 + tc],
                        ALU.mult, ALU.add)
        V.tensor_tensor(H[:, tpo:tpo + 1024], H[:, tpo:tpo + 1024],
                        G[:, 2048 * tc + 1024 * hf:2048 * tc + 1024 * (hf + 1)], ALU.add)
        S.wait("vva")
        S.activation(H[:, tpo:tpo + 1024], H[:, tpo:tpo + 1024], AF.Silu)
        V.wait("ssa")
        V.tensor_tensor(H[:, tpo:tpo + 1024], H[:, tpo:tpo + 1024], H[:, 1024:2048], ALU.mult)
        c_out = V.tensor_tensor(H[:, tpo:tpo + 1024], H[:, tpo:tpo + 1024], H[:, 0:1024], ALU.add)
        qwait(SYNr, "vva", c_out)
        outst.append(dma(SYNr, "ds0" if it % 2 == 0 else "ds1",
                         oo[tc][:, 1024 * hf:1024 * (hf + 1)], H[:, tpo:tpo + 1024]))

    return finish()


def kernel(**inputs):
    nc = build_program()
    in_maps = host_prep(inputs)
    res = run_bass_kernel_spmd(nc, in_maps, core_ids=list(range(8)))
    out = np.empty((B, T, E), dtype=np.float32)
    for c in range(8):
        b, p = c // 2, c % 2
        out[b, own_rows(p)] = res.results[c]["oo"].reshape(R, E)
    return out


# version 05
# speedup vs baseline: 1.0158x; 1.0158x over previous
import os
from contextlib import ExitStack
import numpy as np
import ml_dtypes
import concourse.bass as bass
import concourse.mybir as mybir
from concourse.bass_utils import run_bass_kernel_spmd

F32 = mybir.dt.float32
BF16 = mybir.dt.bfloat16
AF = mybir.ActivationFunctionType
ALU = mybir.AluOpType
AX = mybir.AxisListType

B, T, E, NH, D = 4, 2048, 2048, 16, 128
NEXP, NXP = 32, 34
R = 1024
NEG = -1.0e30
ISQD = float(1.0 / np.sqrt(128.0))
EG = (8, 16)


def own_rows(p):
    if p == 0:
        return list(range(0, 512)) + list(range(1536, 2048))
    return list(range(512, 1536))


def host_prep(inp):
    f = np.ascontiguousarray
    x = np.asarray(inp["L0_attn_x"], dtype=np.float32)
    Wq, bq = np.asarray(inp["L0_attn_Wq"]), np.asarray(inp["L0_attn_bq"])
    Wk, bk = np.asarray(inp["L0_attn_Wk"]), np.asarray(inp["L0_attn_bk"])
    Wv, bv = np.asarray(inp["L0_attn_Wv"]), np.asarray(inp["L0_attn_bv"])
    Wo, bo = np.asarray(inp["L0_attn_Wo"]), np.asarray(inp["L0_attn_bo"])
    nw = np.asarray(inp["L0_ffn_norm_w"])
    rw = np.asarray(inp["L0_ffn_router_w"])
    w1 = np.asarray(inp["L0_ffn_w1"])
    w2 = np.asarray(inp["L0_ffn_w2"])
    sg = np.asarray(inp["L0_ffn_sg"])
    su = np.asarray(inp["L0_ffn_su"])
    sd = np.asarray(inp["L0_ffn_sd"])

    def head_stack(W):
        Wr = W.reshape(NH, 128, 16, 128)
        return f(Wr.transpose(0, 3, 2, 1).reshape(NH, 128, 2048))

    def f_tiles(W):
        Wr = W.reshape(4, 512, 16, 128)
        return f(Wr.transpose(0, 2, 3, 1))

    w1x = np.empty((NXP, 8, 128, 4096), dtype=np.float32)
    w1r = w1.reshape(NEXP, 16, 128, 2, 8, 128)
    w1x[:NEXP] = w1r.transpose(0, 4, 2, 1, 3, 5).reshape(NEXP, 8, 128, 4096)
    sgr = sg.reshape(16, 128, 16, 128)
    sur = su.reshape(16, 128, 16, 128)
    for hh in range(2):
        blk = np.stack([sgr[:, :, 8 * hh:8 * hh + 8, :],
                        sur[:, :, 8 * hh:8 * hh + 8, :]], axis=3)
        w1x[NEXP + hh] = blk.transpose(2, 1, 0, 3, 4).reshape(8, 128, 4096)
    w2x = np.empty((NXP, 8, 128, 2048), dtype=np.float32)
    w2x[:NEXP] = w2.reshape(NEXP, 8, 128, 2048)
    sdr = sd.reshape(16, 128, 2048)
    w2x[NEXP] = sdr[:8]
    w2x[NEXP + 1] = sdr[8:]

    j = np.arange(64, dtype=np.float32)
    thetas = np.float32(10000.0) ** (np.float32(-2.0) * j / np.float32(128.0))
    m_all = np.arange(1, T + 1, dtype=np.float32)
    ang = (m_all[None, :] * thetas[:, None]).astype(np.float32)
    cosf = np.concatenate([np.cos(ang), np.cos(ang)], 0).astype(np.float32)
    sinf = np.concatenate([np.sin(ang), np.sin(ang)], 0).astype(np.float32)

    shared = {
        "wq_h": head_stack(Wq), "wk_h": head_stack(Wk),
        "wv_t": f_tiles(Wv), "wo_t": f_tiles(Wo),
        "bq_d": f(bq.reshape(NH, 128).T), "bk_d": f(bk.reshape(NH, 128).T),
        "bv_d": f(bv.reshape(NH, 128).T), "bo_row": f(bo.reshape(1, E)),
        "nw_d": f(nw.reshape(16, 128).T),
        "rwt": f(rw.reshape(NEXP, 16, 128).transpose(1, 2, 0)),
        "w1x": w1x, "w2x": w2x, "ckc": cosf, "cks": sinf,
    }
    in_maps = []
    for c in range(8):
        b, p = c // 2, c % 2
        rows = own_rows(p)
        xb = x[b]
        m = dict(shared)
        m["xb_fm"] = f(xb.T)
        m["xo_fm"] = f(xb[rows].T)
        m["xo_tm"] = f(xb[rows])
        m["cqc"] = f(cosf[:, rows])
        m["cqs"] = f(sinf[:, rows])
        mask = np.zeros((16, 128, 512), dtype=np.float32)
        qpos = np.asarray(rows)
        for s in range(16):
            g = 0 if s < 8 else 1
            kglob = 128 * s + np.arange(128)
            qglob = qpos[512 * g: 512 * g + 512]
            mask[s] = np.where(kglob[:, None] <= qglob[None, :], 0.0, NEG)
        m["maskst"] = mask.astype(ml_dtypes.bfloat16)
        in_maps.append(m)
    return in_maps


gLGT, gRWT, gBRW1, gBRW2 = 14336, 15360, 10240, 12288


def build_program(taps=(), phase_stop=99):
    nc = bass.Bass("TRN2", target_bir_lowering=False)

    def din(name, shape, dtype=F32):
        return nc.dram_tensor(name, shape, dtype, kind="ExternalInput")

    xb_fm = din("xb_fm", [E, T]); xo_fm = din("xo_fm", [E, R]); xo_tm = din("xo_tm", [R, E])
    wq_h = din("wq_h", [NH, 128, 2048]); wk_h = din("wk_h", [NH, 128, 2048])
    wv_t = din("wv_t", [4, 16, 128, 512]); wo_t = din("wo_t", [4, 16, 128, 512])
    bq_d = din("bq_d", [128, 16]); bk_d = din("bk_d", [128, 16]); bv_d = din("bv_d", [128, 16])
    bo_row_d = din("bo_row", [1, E])
    nw_d = din("nw_d", [128, 16]); rwt_d = din("rwt", [16, 128, 32])
    w1x_d = din("w1x", [NXP, 8, 128, 4096]); w2x_d = din("w2x", [NXP, 8, 128, 2048])
    ckc_d = din("ckc", [128, T]); cks_d = din("cks", [128, T])
    cqc_d = din("cqc", [128, R]); cqs_d = din("cqs", [128, R])
    mask_d = din("maskst", [16, 128, 512], BF16)

    oo = nc.dram_tensor("oo", [8, 128, 2048], F32, kind="ExternalOutput")
    q_dram = nc.dram_tensor("q_dram", [NH, 128, R], BF16, kind="Internal")
    k_dram = nc.dram_tensor("k_dram", [NH, 128, T], BF16, kind="Internal")
    v_dram = nc.dram_tensor("v_dram", [4, 16, 128, 512], BF16, kind="Internal")
    s_dram = nc.dram_tensor("s_dram", [8, 128, 2048], F32, kind="Internal")

    tap_t = {}
    for tname, tshape, tdt in taps:
        tap_t[tname] = nc.dram_tensor("tap_" + tname, tshape, tdt, kind="ExternalOutput")

    ident_c = nc.inline_tensor(np.eye(128, dtype=np.float32), "identc")
    rotm = np.zeros((128, 128), dtype=np.float32)
    for m_ in range(64):
        rotm[m_ + 64, m_] = -1.0
        rotm[m_, m_ + 64] = 1.0
    rot_c = nc.inline_tensor(rotm, "rotc")

    al = nc.alloc_sbuf_tensor
    ones_f = al("ones_f", [128, 8], F32)
    ones_b = al("ones_b", [128, 8], BF16)
    onesr_f = al("onesr_f", [1, 128], F32)
    bqs = al("bqs", [128, 16], F32); bks = al("bks", [128, 16], F32); bvs = al("bvs", [128, 16], F32)
    nws = al("nws", [128, 16], F32)
    stats = al("stats", [128, 32], F32)
    gates = al("gates", [128, 8 * NXP], F32)
    rms_b = al("rms_b", [128, R], F32)
    xnw = al("xnw", [128, 16 * R], BF16)
    RD = al("RD", [128, 16384], BF16)
    RE = al("RE", [128, 12288], BF16)
    RF = al("RF", [128, 16384], BF16)
    G = al("G", [128, 16384], F32)
    H = al("H", [128, 4096], F32)
    rot_s = al("rot_s", [128, 128], F32)
    id_s = al("id_s", [128, 128], F32)
    cst = al("cst", [1, 4], F32)
    assert nc.sbuf_bytes_remaining >= 0, nc.sbuf_bytes_remaining

    ctx = ExitStack()
    sems = {}
    SEMNAMES = ["mmt", "vva", "ssa", "gga",
                "dl0", "dl1", "dg0", "dg1", "dgc", "dsl",
                "ds0", "ds1", "dm0", "dm1", "dw0", "dw1", "dst"]
    for sn in SEMNAMES:
        sems[sn] = ctx.enter_context(nc.semaphore(sn))
    cnt = {k: 0 for k in sems}

    TENr, VECr, SCAr, SYNr, GPSr = nc.tensor, nc.vector, nc.scalar, nc.sync, nc.gpsimd

    class Fenced:
        def __init__(self, eng, sn):
            object.__setattr__(self, "eng", eng)
            object.__setattr__(self, "sn", sn)
        def __getattr__(self, name):
            eng, sn = self.eng, self.sn
            fn = getattr(eng, name)
            def call(*a, **kw):
                eng.wait_ge(sems[sn], cnt[sn])
                ins = fn(*a, **kw)
                ins.then_inc(sems[sn], 1)
                cnt[sn] += 1
                return cnt[sn]
            return call
        def wait(self, sn2, val=None):
            self.eng.wait_ge(sems[sn2], cnt[sn2] if val is None else val)

    V = Fenced(VECr, "vva")
    S = Fenced(SCAr, "ssa")

    def tinc(ins):
        ins.then_inc(sems["mmt"], 1)
        cnt["mmt"] += 1
        return cnt["mmt"]

    def twait(sn, val=None):
        TENr.wait_ge(sems[sn], cnt[sn] if val is None else val)

    def dma(queue, sn, out, in_):
        ins = queue.dma_start(out=out, in_=in_)
        ins.then_inc(sems[sn], 16)
        cnt[sn] += 16
        return cnt[sn]

    def qwait(queue, sn, val=None):
        queue.wait_ge(sems[sn], cnt[sn] if val is None else val)

    def row(off, n):
        return G[0:1, off:off + n]

    def gcol(off, n):
        return G[:, off:off + n]

    def finish():
        qwait(SYNr, "dst"); qwait(SYNr, "ds0"); qwait(SYNr, "ds1")
        ctx.close()
        return nc

    for dst_, src_ in ((bqs[:], bq_d.ap()), (bks[:], bk_d.ap()), (bvs[:], bv_d.ap()),
                       (nws[:], nw_d.ap()), (id_s[:], ident_c.ap()), (rot_s[:], rot_c.ap())):
        dma(SYNr, "dsl", dst_, src_)
    dma(SYNr, "dsl", gcol(gRWT, 512).rearrange("p (k c) -> p k c", k=16),
        rwt_d.ap().rearrange("k p c -> p k c"))
    V.memset(ones_f[:], 1.0)
    V.memset(ones_b[:], 1.0)
    V.memset(onesr_f[:], 1.0)
    V.memset(cst[0:1, 0:1], 1.0e-5)
    V.memset(cst[0:1, 1:2], 1.0e-6)
    V.wait("dsl")
    for k in range(16):
        sl = gcol(gRWT + 32 * k, 32)
        V.tensor_scalar(sl, sl, nws[:, k:k + 1], None, ALU.mult)

    XB = (0, 1024); SQ = (2048, 3072)
    rMU, rVAR, rRSTD, rNEGM, rRMS, rTMP = 4096, 5120, 6144, 7168, 8192, 9216
    BC0, BC1 = 4096, 5120
    p1a = ExitStack()
    with nc.psum_tensor("ps_lgt", [32, 1024], F32) as ps_lgt:
        ps_sum = p1a.enter_context(nc.psum_tensor("ps_sum", [1, 1024], F32))
        ps_sq = p1a.enter_context(nc.psum_tensor("ps_sq", [1, 1024], F32))

        used1, sq1, ld1 = [], [], []
        for k in range(16):
            xb, sq = XB[k % 2], SQ[k % 2]
            par = "dl0" if k % 2 == 0 else "dg1"
            qq = SYNr if k % 2 == 0 else GPSr
            if k >= 2:
                qwait(qq, "mmt", used1[k - 2])
            ld1.append(dma(qq, par, gcol(xb, 1024), xo_fm[128 * k:128 * (k + 1), :]))
            S.wait(par, ld1[k])
            if k >= 2:
                S.wait("mmt", used1[k - 2])
            sq1.append(S.activation(gcol(sq, 1024), gcol(xb, 1024), AF.Square))
            twait("ssa", sq1[k])
            if k == 0:
                twait("vva")
            for h2 in range(2):
                TENr.matmul(ps_sum[:, 512 * h2:512 * (h2 + 1)], ones_f[:, 0:1],
                            gcol(xb + 512 * h2, 512), start=(k == 0), stop=(k == 15))
                ins = TENr.matmul(ps_sq[:, 512 * h2:512 * (h2 + 1)], ones_f[:, 0:1],
                                  gcol(sq + 512 * h2, 512), start=(k == 0), stop=(k == 15))
            used1.append(tinc(ins))

        V.wait("mmt")
        V.tensor_scalar(row(rMU, 1024), ps_sum[0:1, :], 1.0 / 2048.0, None, ALU.mult)
        V.tensor_scalar(row(rVAR, 1024), ps_sq[0:1, :], 1.0 / 2048.0, None, ALU.mult)
        V.tensor_tensor(row(rTMP, 1024), row(rMU, 1024), row(rMU, 1024), ALU.mult)
        V.tensor_tensor(row(rVAR, 1024), row(rVAR, 1024), row(rTMP, 1024), ALU.subtract)
        S.wait("vva")
        S.activation(row(rTMP, 1024), row(rVAR, 1024), AF.Ln, bias=cst[0:1, 0:1])
        S.activation(row(rRSTD, 1024), row(rTMP, 1024), AF.Exp, scale=-0.5)
        V.wait("ssa")
        V.scalar_tensor_tensor(row(rNEGM, 1024), row(rMU, 1024), -1.0, row(rRSTD, 1024), ALU.mult, ALU.mult)
        V.tensor_scalar(row(rTMP, 1024), row(rVAR, 1024), 1.0e-5, None, ALU.add)
        V.reciprocal(row(rMU, 1024), row(rTMP, 1024))
        V.tensor_tensor(row(rVAR, 1024), row(rVAR, 1024), row(rMU, 1024), ALU.mult)
        S.wait("vva")
        S.activation(row(rTMP, 1024), row(rVAR, 1024), AF.Ln, bias=cst[0:1, 1:2])
        S.activation(row(rRMS, 1024), row(rTMP, 1024), AF.Exp, scale=-0.5)
        p1a.close()
        ps_bc = p1a.enter_context(nc.psum_tensor("ps_bc", [128, 1024], F32))
        ps_st = p1a.enter_context(nc.psum_tensor("ps_st", [128, 32], F32))

        twait("ssa")
        for i, roff in enumerate((rRSTD, rNEGM, rRMS)):
            for tc in range(8):
                ins = TENr.matmul(ps_st[:, 8 * i + tc:8 * i + tc + 1], row(roff + 128 * tc, 128),
                                  onesr_f[0:1, 0:1], start=True, stop=True)
        m_st = tinc(ins)
        V.wait("mmt", m_st)
        V.tensor_copy(stats[:, 0:24], ps_st[:, 0:24])

        for src_, dst_ in ((rRMS, rms_b[:, :]), (rRSTD, gcol(BC0, 1024)), (rNEGM, gcol(BC1, 1024))):
            twait("vva")
            for h2 in range(2):
                ins = TENr.matmul(ps_bc[:, 512 * h2:512 * (h2 + 1)], onesr_f[:, :],
                                  row(src_ + 512 * h2, 512), start=True, stop=True)
            m_ = tinc(ins)
            V.wait("mmt", m_)
            V.tensor_copy(dst_, ps_bc[:, :])

        used2, ld2, xn2, vv2 = [], [], [], []
        qwait(SYNr, "ssa"); qwait(SYNr, "mmt"); qwait(SYNr, "vva")
        qwait(GPSr, "ssa"); qwait(GPSr, "mmt"); qwait(GPSr, "vva")
        for k in range(16):
            xb, sc = XB[k % 2], SQ[k % 2]
            par = "dl0" if k % 2 == 0 else "dg1"
            qq = SYNr if k % 2 == 0 else GPSr
            if k >= 2:
                qwait(qq, "vva", vv2[k - 2])
            ld2.append(dma(qq, par, gcol(xb, 1024), xo_fm[128 * k:128 * (k + 1), :]))
            V.wait(par, ld2[k])
            if k >= 2:
                V.wait("mmt", used2[k - 2])
                V.wait("ssa", xn2[k - 2])
            c_tt = V.tensor_tensor(gcol(sc, 1024), gcol(xb, 1024), gcol(BC0, 1024), ALU.mult)
            vv2.append(c_tt)
            V.tensor_tensor(gcol(sc, 1024), gcol(sc, 1024), gcol(BC1, 1024), ALU.add)
            c_xnw = V.tensor_scalar(xnw[:, R * k:R * (k + 1)], gcol(sc, 1024), nws[:, k:k + 1], None, ALU.mult)
            twait("vva", c_xnw)
            TENr.matmul(ps_lgt[:, 0:512], gcol(gRWT + 32 * k, 32), gcol(sc, 512), start=(k == 0), stop=(k == 15))
            ins = TENr.matmul(ps_lgt[:, 512:1024], gcol(gRWT + 32 * k, 32), gcol(sc + 512, 512), start=(k == 0), stop=(k == 15))
            used2.append(tinc(ins))
            S.wait("vva", c_xnw)
            xn2.append(S.copy(RD[:, R * k:R * (k + 1)], gcol(sc, 1024)))

        V.wait("mmt")
        V.tensor_copy(G[0:32, gLGT:gLGT + 1024], ps_lgt[:, :])
        p1a.close()

    def tap(name, src_ap):
        if name in tap_t:
            for sn_ in ("vva", "ssa", "mmt", "gga", "ds0", "ds1", "dgc", "dst"):
                qwait(SYNr, sn_)
            dma(SYNr, "dst", tap_t[name].ap(), src_ap)

    tap("xnw", xnw[:, :])
    tap("xn_own", RD[:, 0:16 * R])
    tap("lgt", G[0:32, gLGT:gLGT + 1024])
    tap("stats", stats[:, 0:24])
    tap("rmsb", rms_b[:, :])
    if phase_stop <= 1:
        return finish()

    bMU, bVAR, bTMP = 0, 2048, 4096
    with nc.psum_tensor("ps_sum2", [1, 2048], F32) as ps_sum2, \
         nc.psum_tensor("ps_sq2", [1, 2048], F32) as ps_sq2:
        usedb, sqb, ldb = [], [], []
        for k in range(16):
            xb, sq = 2048 * (k % 2), 4096 + 2048 * (k % 2)
            par = "dg0" if k % 2 == 0 else "dg1"
            if k >= 2:
                qwait(GPSr, "mmt", usedb[k - 2])
            ldb.append(dma(GPSr, par, RE[:, xb:xb + 2048], xb_fm[128 * k:128 * (k + 1), :]))
            S.wait(par, ldb[k])
            if k >= 2:
                S.wait("mmt", usedb[k - 2])
            sqb.append(S.activation(RE[:, sq:sq + 2048], RE[:, xb:xb + 2048], AF.Square))
            twait("ssa", sqb[k])
            if k == 0:
                twait("vva")
            for h4 in range(4):
                TENr.matmul(ps_sum2[:, 512 * h4:512 * (h4 + 1)], ones_b[:, 0:1],
                            RE[:, xb + 512 * h4:xb + 512 * (h4 + 1)], start=(k == 0), stop=(k == 15))
                ins = TENr.matmul(ps_sq2[:, 512 * h4:512 * (h4 + 1)], ones_b[:, 0:1],
                                  RE[:, sq + 512 * h4:sq + 512 * (h4 + 1)], start=(k == 0), stop=(k == 15))
            usedb.append(tinc(ins))

        V.wait("mmt")
        V.tensor_scalar(row(bMU, 2048), ps_sum2[0:1, :], 1.0 / 2048.0, None, ALU.mult)
        V.tensor_scalar(row(bVAR, 2048), ps_sq2[0:1, :], 1.0 / 2048.0, None, ALU.mult)
        V.tensor_tensor(row(bTMP, 2048), row(bMU, 2048), row(bMU, 2048), ALU.mult)
        V.tensor_tensor(row(bVAR, 2048), row(bVAR, 2048), row(bTMP, 2048), ALU.subtract)
        S.wait("vva")
        S.activation(row(bTMP, 2048), row(bVAR, 2048), AF.Ln, bias=cst[0:1, 0:1])
        S.activation(row(gBRW1, 2048), row(bTMP, 2048), AF.Exp, scale=-0.5)
        V.wait("ssa")
        V.scalar_tensor_tensor(row(gBRW2, 2048), row(bMU, 2048), -1.0, row(gBRW1, 2048), ALU.mult, ALU.mult)

    tap("brw1", row(gBRW1, 2048))
    tap("brw2", row(gBRW2, 2048))
    if phase_stop <= 2:
        return finish()

    lgS, swS, scS, gnS = 0, 32, 64, 128
    mxS, m4S, dnS = 96, 97, 98
    with nc.psum_tensor("ps_t7", [128, 32], F32) as ps_t7:
        for tc in range(8):
            twait("vva")
            ins = TENr.transpose(ps_t7[:, 0:32], G[0:32, gLGT + 128 * tc:gLGT + 128 * (tc + 1)], id_s[0:32, 0:32])
            m_ = tinc(ins)
            V.wait("mmt", m_)
            V.tensor_copy(gcol(lgS, 32), ps_t7[:, 0:32])
            V.tensor_scalar(gcol(lgS, 32), gcol(lgS, 32), stats[:, 16 + tc:17 + tc], None, ALU.mult)
            V.reduce_max(gcol(mxS, 1), gcol(lgS, 32), axis=AX.X, negate=True)
            S.wait("vva")
            S.activation(gcol(scS, 32), gcol(lgS, 32), AF.Exp, bias=gcol(mxS, 1))
            V.wait("ssa")
            V.tensor_copy(gcol(swS, 32), gcol(scS, 32))
            for r_ in range(3):
                V.reduce_max(gcol(m4S, 1), gcol(swS, 32), axis=AX.X)
                V.tensor_scalar(gcol(gnS, 32), gcol(swS, 32), gcol(m4S, 1), None, ALU.is_ge)
                V.scalar_tensor_tensor(gcol(swS, 32), gcol(gnS, 32), NEG, gcol(swS, 32), ALU.mult, ALU.add)
            V.reduce_max(gcol(m4S, 1), gcol(swS, 32), axis=AX.X)
            V.tensor_scalar(gcol(gnS, 32), gcol(scS, 32), gcol(m4S, 1), None, ALU.is_ge)
            V.tensor_tensor(gcol(gnS, 32), gcol(scS, 32), gcol(gnS, 32), ALU.mult)
            V.reduce_sum(gcol(dnS, 1), gcol(gnS, 32), axis=AX.X)
            V.reciprocal(gcol(m4S, 1), gcol(dnS, 1))
            V.tensor_scalar(gcol(gnS, 32), gcol(gnS, 32), gcol(m4S, 1), None, ALU.mult)
            V.tensor_scalar(gates[:, NXP * tc:NXP * tc + 32], gcol(gnS, 32), stats[:, 16 + tc:17 + tc], None, ALU.mult)
            V.tensor_copy(gates[:, NXP * tc + 32:NXP * tc + 33], stats[:, 16 + tc:17 + tc])
            V.tensor_copy(gates[:, NXP * tc + 33:NXP * tc + 34], stats[:, 16 + tc:17 + tc])

    tap("gates", gates[:, :])
    if phase_stop <= 3:
        return finish()

    cqcS, cqsS, qfS, t1S, t2S = 0, 1024, 2048, 4096, 5120
    qwait(SYNr, "vva"); qwait(SYNr, "ssa")
    dma(SYNr, "dsl", gcol(cqcS, 1024), cqc_d.ap())
    dma(SYNr, "dsl", gcol(cqsS, 1024), cqs_d.ap())
    with nc.psum_tensor("ps_q", [128, 2 * 1024], F32) as ps_q, \
         nc.psum_tensor("ps_r", [128, 2 * 1024], F32) as ps_r:
        qmm, qev_bias, qev_rd, qev_q, qst = [], [], [], [], []
        qwait(GPSr, "mmt"); qwait(GPSr, "ssa"); qwait(GPSr, "vva")
        qwait(SYNr, "mmt")

        def q_stageA(h):
            par = "dg0" if h % 2 == 0 else "dg1"
            ofs = 2048 * (h % 2)
            if h >= 2:
                qwait(GPSr, "mmt", qmm[h - 2])
            wql = dma(GPSr, par, RE[:, ofs:ofs + 2048], wq_h[h])
            twait(par, wql)
            if h == 0:
                twait("ssa"); twait("dsl"); twait("vva")
            if h >= 2:
                twait("vva", qev_bias[h - 2])
            for k in range(16):
                for h2 in range(2):
                    ins = TENr.matmul(ps_q[:, 1024 * (h % 2) + 512 * h2:1024 * (h % 2) + 512 * (h2 + 1)],
                                      RE[:, ofs + 128 * k:ofs + 128 * (k + 1)],
                                      RD[:, R * k + 512 * h2:R * k + 512 * (h2 + 1)],
                                      start=(k == 0), stop=(k == 15))
            qmm.append(tinc(ins))
            V.wait("mmt", qmm[h])
            if h >= 2:
                V.wait("mmt", qrot_mm[h - 2] if len(qrot_mm) > h - 2 else 0)
            qev_bias.append(V.tensor_scalar(gcol(qfS + 1024 * (h % 2), 1024),
                                            ps_q[:, 1024 * (h % 2):1024 * (h % 2) + 1024],
                                            bqs[:, h:h + 1], None, ALU.add))

        qrot_mm = []

        def q_stageB(h):
            twait("vva", qev_bias[h])
            if h >= 2:
                twait("vva", qev_rd[h - 2])
            for h2 in range(2):
                ins = TENr.matmul(ps_r[:, 1024 * (h % 2) + 512 * h2:1024 * (h % 2) + 512 * (h2 + 1)],
                                  rot_s[:, :], gcol(qfS + 1024 * (h % 2) + 512 * h2, 512), start=True, stop=True)
            qrot_mm.append(tinc(ins))
            V.wait("mmt", qrot_mm[h])
            V.tensor_tensor(gcol(t1S, 1024), gcol(qfS + 1024 * (h % 2), 1024), gcol(cqcS, 1024), ALU.mult)
            qev_rd.append(V.tensor_tensor(gcol(t2S, 1024), ps_r[:, 1024 * (h % 2):1024 * (h % 2) + 1024],
                                          gcol(cqsS, 1024), ALU.mult))
            spar = "ds0" if h % 2 == 0 else "ds1"
            if h >= 2:
                V.wait(spar, qst[h - 2])
            qev_q.append(V.tensor_tensor(RE[:, 4096 + 1024 * (h % 2):4096 + 1024 * (h % 2) + 1024],
                                         gcol(t1S, 1024), gcol(t2S, 1024), ALU.add))
            qwait(SYNr, "vva", qev_q[h])
            qst.append(dma(SYNr, spar, q_dram[h], RE[:, 4096 + 1024 * (h % 2):4096 + 1024 * (h % 2) + 1024]))

        for h in range(NH):
            q_stageA(h)
            if h >= 1:
                q_stageB(h - 1)
        q_stageB(NH - 1)
    tap("q_dram", q_dram.ap())
    if phase_stop <= 4:
        return finish()

    ckcS, cksS, kfS, t1S4, BCh0, BCh1, t2S4 = 0, 2048, 4096, 6144, 6144, 7168, 8192
    qwait(SYNr, "vva")
    dma(SYNr, "dsl", gcol(ckcS, 2048), ckc_d.ap())
    dma(SYNr, "dsl", gcol(cksS, 2048), cks_d.ap())
    with nc.psum_tensor("ps_k", [128, 2 * 1024], F32) as ps_k, \
         nc.psum_tensor("ps_kr", [128, 1024], F32) as ps_kr, \
         nc.psum_tensor("ps_v", [128, 2 * 512], F32) as ps_v:
        ps_bch = ps_v
        for hh in range(2):
            chs = slice(1024 * hh, 1024 * (hh + 1))
            twait("vva"); twait("ssa")
            for h2 in range(2):
                ins = TENr.matmul(ps_bch[:, 512 * h2:512 * (h2 + 1)], onesr_f[:, :],
                                  row(gBRW1 + 1024 * hh + 512 * h2, 512), start=True, stop=True)
            m_ = tinc(ins)
            V.wait("mmt", m_)
            V.tensor_copy(gcol(BCh0, 1024), ps_bch[:, :])
            twait("vva")
            for h2 in range(2):
                ins = TENr.matmul(ps_bch[:, 512 * h2:512 * (h2 + 1)], onesr_f[:, :],
                                  row(gBRW2 + 1024 * hh + 512 * h2, 512), start=True, stop=True)
            m_ = tinc(ins)
            V.wait("mmt", m_)
            V.tensor_copy(gcol(BCh1, 1024), ps_bch[:, :])
            xnb, xbld = [], []
            qwait(GPSr, "vva"); qwait(GPSr, "ssa"); qwait(GPSr, "mmt")
            qwait(SYNr, "vva"); qwait(SYNr, "ssa"); qwait(SYNr, "mmt")
            V.wait("mmt")
            for k in range(16):
                par = "dg0" if k % 2 == 0 else "dg1"
                ofs = 2048 * (k % 2)
                if k >= 2:
                    qwait(GPSr, "vva", xnb[k - 2])
                xbld.append(dma(GPSr, par, RE[:, ofs:ofs + 1024], xb_fm[128 * k:128 * (k + 1), chs]))
                V.wait(par, xbld[k])
                V.tensor_tensor(gcol(t2S4, 1024), RE[:, ofs:ofs + 1024], gcol(BCh0, 1024), ALU.mult)
                xnb.append(V.tensor_tensor(RD[:, 1024 * k:1024 * (k + 1)], gcol(t2S4, 1024), gcol(BCh1, 1024), ALU.add))
            kmm, kev_bias, kev_rd, kev_k, kst, krot_mm = [], [], [], [], [], []
            V.wait("ds0"); V.wait("ds1")
            qwait(GPSr, "vva")

            def k_stageA(h):
                par = "dg0" if h % 2 == 0 else "dg1"
                ofs = 2048 * (h % 2)
                if h >= 2:
                    qwait(GPSr, "mmt", kmm[h - 2])
                wkl = dma(GPSr, par, RE[:, ofs:ofs + 2048], wk_h[h])
                twait(par, wkl)
                if h == 0:
                    twait("vva"); twait("dsl")
                if h >= 2:
                    twait("vva", kev_bias[h - 2])
                for k in range(16):
                    for h2 in range(2):
                        ins = TENr.matmul(ps_k[:, 1024 * (h % 2) + 512 * h2:1024 * (h % 2) + 512 * (h2 + 1)],
                                          RE[:, ofs + 128 * k:ofs + 128 * (k + 1)],
                                          RD[:, 1024 * k + 512 * h2:1024 * k + 512 * (h2 + 1)],
                                          start=(k == 0), stop=(k == 15))
                kmm.append(tinc(ins))
                V.wait("mmt", kmm[h])
                if h >= 2:
                    V.wait("mmt", krot_mm[h - 2])
                kev_bias.append(V.tensor_scalar(gcol(kfS + 1024 * (h % 2), 1024),
                                                ps_k[:, 1024 * (h % 2):1024 * (h % 2) + 1024],
                                                bks[:, h:h + 1], None, ALU.add))

            def k_stageB(h):
                twait("vva", kev_bias[h])
                if h >= 1:
                    twait("vva", kev_rd[h - 1])
                for h2 in range(2):
                    ins = TENr.matmul(ps_kr[:, 512 * h2:512 * (h2 + 1)],
                                      rot_s[:, :], gcol(kfS + 1024 * (h % 2) + 512 * h2, 512), start=True, stop=True)
                krot_mm.append(tinc(ins))
                V.wait("mmt", krot_mm[h])
                V.tensor_tensor(gcol(t1S4, 1024), gcol(kfS + 1024 * (h % 2), 1024), gcol(ckcS + 1024 * hh, 1024), ALU.mult)
                kev_rd.append(V.tensor_tensor(gcol(t2S4, 1024), ps_kr[:, 0:1024],
                                              gcol(cksS + 1024 * hh, 1024), ALU.mult))
                spar = "ds0" if h % 2 == 0 else "ds1"
                if h >= 2:
                    V.wait(spar, kst[h - 2])
                kev_k.append(V.tensor_tensor(RE[:, 4096 + 1024 * (h % 2):4096 + 1024 * (h % 2) + 1024],
                                             gcol(t1S4, 1024), gcol(t2S4, 1024), ALU.add))
                qwait(SYNr, "vva", kev_k[h])
                kst.append(dma(SYNr, spar, k_dram[h][:, chs],
                               RE[:, 4096 + 1024 * (h % 2):4096 + 1024 * (h % 2) + 1024]))

            for h in range(NH):
                k_stageA(h)
                if h >= 1:
                    k_stageB(h - 1)
            k_stageB(NH - 1)
            vst = []
            vmm = []
            for ft in range(4):
                ofsF = 8192 * (ft % 2)
                if ft >= 2:
                    qwait(GPSr, "mmt", vmm[ft - 2])
                if ft == 0 and hh == 1:
                    qwait(GPSr, "mmt")
                wvl = dma(GPSr, "dgc", RF[:, ofsF:ofsF + 8192].rearrange("p (k c) -> p k c", k=16),
                          wv_t[ft].rearrange("k p c -> p k c"))
                twait("dgc", wvl)
                if ft == 0:
                    twait("vva")
                for tk in range(8):
                    pv = tk % 2
                    nn = len(vst)
                    if nn >= 2:
                        twait("vva", vst[nn - 2][0])
                    for k in range(16):
                        ins = TENr.matmul(ps_v[:, 512 * pv:512 * (pv + 1)],
                                          RD[:, 1024 * k + 128 * tk:1024 * k + 128 * (tk + 1)],
                                          RF[:, ofsF + 512 * k:ofsF + 512 * (k + 1)],
                                          start=(k == 0), stop=(k == 15))
                    m_v = tinc(ins)
                    V.wait("mmt", m_v)
                    spar = "ds0" if nn % 2 == 0 else "ds1"
                    if nn >= 2:
                        V.wait(spar, vst[nn - 2][1])
                    c_v = V.tensor_copy(RE[:, 8192 + 512 * pv:8192 + 512 * (pv + 1)], ps_v[:, 512 * pv:512 * (pv + 1)])
                    qwait(SYNr, "vva", c_v)
                    st_ = dma(SYNr, spar, v_dram[ft, 8 * hh + tk], RE[:, 8192 + 512 * pv:8192 + 512 * (pv + 1)])
                    vst.append((c_v, st_))
                vmm.append(cnt["mmt"])
    tap("k_dram", k_dram.ap())
    tap("v_dram", v_dram.ap())
    if phase_stop <= 5:
        return finish()

    PTo, KHo, QHo, MKo = 0, 8192, 12288, 14336
    linvS, lbS, yscrS = 0, 1024, 2048
    with nc.psum_tensor("ps_s", [128, 2 * 1024], F32) as ps_s, \
         nc.psum_tensor("ps_l2", [128, 1024], F32) as ps_l2, \
         nc.psum_tensor("ps_y", [128, 1024], F32) as ps_y:
        qwait(SYNr, "vva"); qwait(SYNr, "ssa"); qwait(SYNr, "mmt")
        qwait(SYNr, "ds0"); qwait(SYNr, "ds1")
        qwait(GPSr, "ds0"); qwait(GPSr, "ds1")
        khld, qhld, st_done, yevs, lrec, lbcp = [], [], [], [], [], []
        exps_pairs = []
        vgld = [None] * 4
        mskld, mskcons = [], []
        for h in range(NH):
            par = "dl0" if h % 2 == 0 else "dl1"
            kofs = KHo + 2048 * (h % 2)
            qofs = QHo + 1024 * (h % 2)
            if h >= 2:
                qwait(SYNr, "mmt", st_done[h - 2])
            khld.append(dma(SYNr, par, RD[:, kofs:kofs + 2048], k_dram[h]))
            qhld.append(dma(SYNr, par, RD[:, qofs:qofs + 1024], q_dram[h]))
            if h % 4 == 0:
                qwait(GPSr, "mmt")
                for tkv in range(16):
                    vg_m = dma(GPSr, "dgc", RE[:, 512 * tkv:512 * (tkv + 1)], v_dram[h // 4, tkv])
                vgld[h // 4] = vg_m
            pv_done = None
            for g01 in range(2):
                Eg = EG[g01]
                exp_marks = []
                for pr in range(Eg // 2):
                    jt0 = 2 * pr
                    slot = len(exps_pairs) % 2
                    twait(par, qhld[h])
                    if pr == 0 and g01 == 0 and h == 0:
                        twait("vva"); twait("ssa")
                    if pr == 0 and g01 == 0 and h % 4 == 0:
                        twait("dgc", vgld[h // 4])
                    if len(exps_pairs) >= 2:
                        twait("ssa", exps_pairs[-2])
                    for dj in range(2):
                        ins = TENr.matmul(ps_s[:, 1024 * slot + 512 * dj:1024 * slot + 512 * (dj + 1)],
                                          RD[:, kofs + 128 * (jt0 + dj):kofs + 128 * (jt0 + dj + 1)],
                                          RD[:, qofs + 512 * g01:qofs + 512 * (g01 + 1)],
                                          start=True, stop=True)
                    m_s = tinc(ins)
                    masked = (g01 == 0) or (jt0 >= 8)
                    if g01 == 1 and pr == 0:
                        S.wait("mmt", pv_done)
                    if g01 == 0 and pr == 0 and h >= 1:
                        S.wait("mmt", st_done[h - 1])
                    if masked:
                        mpar = "dm0" if len(mskld) % 2 == 0 else "dm1"
                        if len(mskld) >= 2:
                            qwait(SYNr, "vva", mskcons[len(mskld) - 2])
                        mof = MKo + 1024 * (len(mskld) % 2)
                        mskld.append(dma(SYNr, mpar,
                                         RD[:, mof:mof + 1024].rearrange("p (s c) -> p s c", s=2),
                                         mask_d.ap()[jt0:jt0 + 2].rearrange("s p c -> p s c")))
                        V.wait(mpar, mskld[-1])
                        V.wait("mmt", m_s)
                        c_m = V.tensor_tensor(ps_s[:, 1024 * slot:1024 * (slot + 1)],
                                              ps_s[:, 1024 * slot:1024 * (slot + 1)],
                                              RD[:, mof:mof + 1024], ALU.add)
                        mskcons.append(c_m)
                        S.wait("vva", c_m)
                    else:
                        S.wait("mmt", m_s)
                    em = S.activation(RD[:, PTo + 512 * jt0:PTo + 512 * (jt0 + 2)],
                                      ps_s[:, 1024 * slot:1024 * (slot + 1)], AF.Exp, scale=ISQD)
                    exp_marks.append(em)
                    exps_pairs.append(em)
                if g01 == 0:
                    if h >= 1:
                        twait("vva", lrec[h - 1])
                        twait("vva", yevs[h - 1])
                        twait("vva", lbcp[h - 1])
                for jt in range(Eg):
                    twait("ssa", exp_marks[jt // 2])
                    TENr.matmul(ps_l2[0:1, 512 * g01:512 * (g01 + 1)], ones_b[:, 0:1],
                                RD[:, PTo + 512 * jt:PTo + 512 * (jt + 1)],
                                start=(jt == 0), stop=(jt == Eg - 1))
                    ins = TENr.matmul(ps_y[:, 512 * g01:512 * (g01 + 1)],
                                      RE[:, 512 * jt + 128 * (h % 4):512 * jt + 128 * (h % 4) + 128],
                                      RD[:, PTo + 512 * jt:PTo + 512 * (jt + 1)],
                                      start=(jt == 0), stop=(jt == Eg - 1))
                pv_done = tinc(ins)
            st_done.append(pv_done)
            V.wait("mmt", st_done[h])
            c_li = V.reciprocal(row(linvS, 1024), ps_l2[0:1, :])
            lrec.append(c_li)
            twait("vva", c_li)
            for h2 in range(2):
                ins = TENr.matmul(ps_l2[:, 512 * h2:512 * (h2 + 1)], onesr_f[:, :],
                                  row(linvS + 512 * h2, 512), start=True, stop=True)
            m_lb = tinc(ins)
            V.wait("mmt", m_lb)
            lbcp.append(V.tensor_copy(gcol(lbS, 1024), ps_l2[:, :]))
            V.tensor_tensor(gcol(yscrS, 1024), ps_y[:, :], gcol(lbS, 1024), ALU.mult)
            yevs.append(V.tensor_scalar(RF[:, 1024 * h:1024 * (h + 1)], gcol(yscrS, 1024),
                                        bvs[:, h:h + 1], None, ALU.add))
    tap("y_fm", RF[:, 0:16384])
    if phase_stop <= 6:
        return finish()

    boS, borowS = 0, 2048
    with nc.psum_tensor("ps_o", [128, 2 * 512], F32) as ps_o, \
         nc.psum_tensor("ps_bo", [128, 2048], F32) as ps_bo:
        qwait(SYNr, "vva")
        dma(SYNr, "dsl", row(borowS, 2048), bo_row_d.ap())
        twait("dsl"); twait("vva")
        for q4 in range(4):
            ins = TENr.matmul(ps_bo[:, 512 * q4:512 * (q4 + 1)], onesr_f[:, :],
                              row(borowS + 512 * q4, 512), start=True, stop=True)
        m_bo = tinc(ins)
        V.wait("mmt", m_bo)
        V.tensor_copy(gcol(boS, 2048), ps_bo[:, :])
        ost = []
        womm = []
        for ft in range(4):
            ofsF = 8192 * (ft % 2)
            if ft >= 2:
                qwait(GPSr, "mmt", womm[ft - 2])
            if ft == 0:
                qwait(GPSr, "mmt"); qwait(GPSr, "vva")
            wol = dma(GPSr, "dgc", RD[:, ofsF:ofsF + 8192].rearrange("p (k c) -> p k c", k=16),
                      wo_t[ft].rearrange("k p c -> p k c"))
            twait("dgc", wol)
            for tc in range(8):
                pv = tc % 2
                nn = len(ost)
                if nn >= 2:
                    twait("vva", ost[nn - 2][0])
                for k in range(16):
                    ins = TENr.matmul(ps_o[:, 512 * pv:512 * (pv + 1)],
                                      RF[:, 1024 * k + 128 * tc:1024 * k + 128 * (tc + 1)],
                                      RD[:, ofsF + 512 * k:ofsF + 512 * (k + 1)],
                                      start=(k == 0), stop=(k == 15))
                m_o = tinc(ins)
                V.wait("mmt", m_o)
                spar = "ds0" if nn % 2 == 0 else "ds1"
                if nn >= 2:
                    V.wait(spar, ost[nn - 2][1])
                c_o = V.tensor_tensor(H[:, 512 * pv:512 * (pv + 1)], ps_o[:, 512 * pv:512 * (pv + 1)],
                                      gcol(boS + 512 * ft, 512), ALU.add)
                qwait(SYNr, "vva", c_o)
                st_ = dma(SYNr, spar, s_dram[tc][:, 512 * ft:512 * (ft + 1)], H[:, 512 * pv:512 * (pv + 1)])
                ost.append((c_o, st_))
            womm.append(cnt["mmt"])
    tap("s_dram", s_dram.ap())
    if phase_stop <= 7:
        return finish()

    with nc.psum_tensor("ps_g", [128, 2 * 512], F32) as ps_g, \
         nc.psum_tensor("ps_u", [128, 2 * 512], F32) as ps_u, \
         nc.psum_tensor("ps_w2", [128, 2 * 512], F32) as ps_w2:
        V.wait("mmt"); V.wait("ssa"); V.wait("ds0"); V.wait("ds1"); V.wait("dst")
        V.memset(G[:, 0:8192], 0.0)
        V.memset(G[:, 8192:16384], 0.0)
        for tc in range(8):
            V.tensor_scalar(G[:, 2048 * tc:2048 * (tc + 1)], G[:, 2048 * tc:2048 * (tc + 1)],
                            stats[:, 8 + tc:9 + tc], None, ALU.add)
        w1mm = {}
        w1ld = []
        evac = {}
        w2mm = [0] * NXP
        w2fhmm = {}
        sttd = []
        qwait(GPSr, "vva"); qwait(GPSr, "ssa"); qwait(GPSr, "mmt")
        for e in range(NXP):
            for jj in range(8):
                gi = 8 * e + jj
                par = "dg0" if gi % 2 == 0 else "dg1"
                ofs = 8192 + 4096 * (gi % 2)
                if gi >= 2:
                    qwait(GPSr, "mmt", w1mm[(gi - 2, 1)])
                w1ld.append(dma(GPSr, par, RD[:, ofs:ofs + 4096], w1x_d[e, jj]))
                twait(par, w1ld[gi])
                for th in range(2):
                    if gi == 0 and th == 0:
                        twait("vva"); twait("ssa")
                    if gi >= 1:
                        twait("vva", evac[(gi - 1, th)])
                    if jj == 0 and th == 0 and e >= 1:
                        twait("mmt", w2mm[e - 1])
                    for half, ps in ((0, ps_g), (1, ps_u)):
                        for k in range(16):
                            ins = TENr.matmul(ps[:, 512 * th:512 * (th + 1)],
                                              RD[:, ofs + 256 * k + 128 * half:ofs + 256 * k + 128 * (half + 1)],
                                              xnw[:, R * k + 512 * th:R * k + 512 * (th + 1)],
                                              start=(k == 0), stop=(k == 15))
                    w1mm[(gi, th)] = tinc(ins)
                    V.wait("mmt", w1mm[(gi, th)])
                    V.tensor_tensor(H[:, 0:512], ps_g[:, 512 * th:512 * (th + 1)],
                                    rms_b[:, 512 * th:512 * (th + 1)], ALU.mult)
                    S.wait("vva")
                    S.activation(H[:, 512:1024], H[:, 0:512], AF.Silu)
                    V.wait("ssa")
                    evac[(gi, th)] = V.tensor_tensor(RD[:, 1024 * jj + 512 * th:1024 * jj + 512 * (th + 1)],
                                                     H[:, 512:1024], ps_u[:, 512 * th:512 * (th + 1)], ALU.mult)
            for fh in range(2):
                gi2 = 2 * e + fh
                ofs2 = 8192 * (gi2 % 2)
                wpar = "dw0" if gi2 % 2 == 0 else "dw1"
                if e >= 1:
                    qwait(GPSr, "mmt", w2fhmm[(e - 1, fh)])
                w2l = dma(GPSr, wpar, RF[:, ofs2:ofs2 + 8192].rearrange("p (i c) -> p i c", i=8),
                          w2x_d[e][:, :, 1024 * fh:1024 * (fh + 1)].rearrange("i p c -> p i c"))
                twait(wpar, w2l)
                twait("vva", evac[(8 * e + 7, 1)])
                for tc in range(8):
                    for f2 in range(2):
                        sidx = len(sttd)
                        pv = sidx % 2
                        if sidx >= 2:
                            twait("vva", sttd[sidx - 2])
                        for i in range(8):
                            ins = TENr.matmul(ps_w2[:, 512 * pv:512 * (pv + 1)],
                                              RD[:, 1024 * i + 128 * tc:1024 * i + 128 * (tc + 1)],
                                              RF[:, ofs2 + 1024 * i + 512 * f2:ofs2 + 1024 * i + 512 * (f2 + 1)],
                                              start=(i == 0), stop=(i == 7))
                        m_w2 = tinc(ins)
                        V.wait("mmt", m_w2)
                        go = 2048 * tc + 1024 * fh + 512 * f2
                        sttd.append(V.scalar_tensor_tensor(gcol(go, 512), ps_w2[:, 512 * pv:512 * (pv + 1)],
                                                           gates[:, NXP * tc + e:NXP * tc + e + 1],
                                                           gcol(go, 512), ALU.mult, ALU.add))
                w2fhmm[(e, fh)] = cnt["mmt"]
            w2mm[e] = cnt["mmt"]
    tap("routed", G[:, :])
    if phase_stop <= 8:
        return finish()

    xld, sld = [], []
    steps4 = []
    qwait(SYNr, "vva", evac[(8 * NXP - 1, 1)])
    qwait(GPSr, "vva", evac[(8 * NXP - 1, 1)])
    for it in range(16):
        tc, hf = it // 2, it % 2
        xb = H[:, 2048 * (it % 2):2048 * (it % 2) + 1024]
        sb = H[:, 2048 * (it % 2) + 1024:2048 * (it % 2) + 2048]
        if it >= 2:
            qwait(GPSr, "vva", steps4[it - 2])
            qwait(SYNr, "vva", steps4[it - 2])
        xld.append(dma(GPSr, "dl0", xb, xo_tm[128 * tc:128 * (tc + 1), 1024 * hf:1024 * (hf + 1)]))
        sld.append(dma(SYNr, "dl1", sb, s_dram[tc][:, 1024 * hf:1024 * (hf + 1)]))
        V.wait("dl0", xld[-1]); V.wait("dl1", sld[-1])
        Gc = G[:, 1024 * it:1024 * (it + 1)]
        V.scalar_tensor_tensor(Gc, xb, stats[:, tc:tc + 1], Gc, ALU.mult, ALU.add)
        S.wait("vva")
        S.activation(Gc, Gc, AF.Silu)
        V.wait("ssa")
        V.tensor_tensor(Gc, Gc, sb, ALU.mult)
        steps4.append(V.tensor_tensor(Gc, Gc, xb, ALU.add))
        qwait(SCAr, "vva", steps4[-1])
        dma(SCAr, "ds0" if it % 2 == 0 else "ds1", oo[tc][:, 1024 * hf:1024 * (hf + 1)], Gc)

    return finish()


def kernel(**inputs):
    nc = build_program()
    in_maps = host_prep(inputs)
    res = run_bass_kernel_spmd(nc, in_maps, core_ids=list(range(8)))
    out = np.empty((B, T, E), dtype=np.float32)
    for c in range(8):
        b, p = c // 2, c % 2
        out[b, own_rows(p)] = res.results[c]["oo"].reshape(R, E)
    return out


# version 06 (final)
# speedup vs baseline: 1.0403x; 1.0241x over previous
import os
from contextlib import ExitStack
import numpy as np
import ml_dtypes
import concourse.bass as bass
import concourse.mybir as mybir
from concourse.bass_utils import run_bass_kernel_spmd

F32 = mybir.dt.float32
BF16 = mybir.dt.bfloat16
AF = mybir.ActivationFunctionType
ALU = mybir.AluOpType
AX = mybir.AxisListType

B, T, E, NH, D = 4, 2048, 2048, 16, 128
NEXP, NXP = 32, 34
R = 1024
NEG = -1.0e30
ISQD = float(1.0 / np.sqrt(128.0))
EG = (8, 16)


def own_rows(p):
    if p == 0:
        return list(range(0, 512)) + list(range(1536, 2048))
    return list(range(512, 1536))


def host_prep(inp):
    f = np.ascontiguousarray
    x = np.asarray(inp["L0_attn_x"], dtype=np.float32)
    Wq, bq = np.asarray(inp["L0_attn_Wq"]), np.asarray(inp["L0_attn_bq"])
    Wk, bk = np.asarray(inp["L0_attn_Wk"]), np.asarray(inp["L0_attn_bk"])
    Wv, bv = np.asarray(inp["L0_attn_Wv"]), np.asarray(inp["L0_attn_bv"])
    Wo, bo = np.asarray(inp["L0_attn_Wo"]), np.asarray(inp["L0_attn_bo"])
    nw = np.asarray(inp["L0_ffn_norm_w"])
    rw = np.asarray(inp["L0_ffn_router_w"])
    w1 = np.asarray(inp["L0_ffn_w1"])
    w2 = np.asarray(inp["L0_ffn_w2"])
    sg = np.asarray(inp["L0_ffn_sg"])
    su = np.asarray(inp["L0_ffn_su"])
    sd = np.asarray(inp["L0_ffn_sd"])

    def head_stack(W):
        Wr = W.reshape(NH, 128, 16, 128)
        return f(Wr.transpose(0, 3, 2, 1).reshape(NH, 128, 2048))

    def f_tiles(W):
        Wr = W.reshape(4, 512, 16, 128)
        return f(Wr.transpose(0, 2, 3, 1))

    w1x = np.empty((NXP, 8, 128, 4096), dtype=np.float32)
    w1r = w1.reshape(NEXP, 16, 128, 2, 8, 128)
    w1x[:NEXP] = w1r.transpose(0, 4, 2, 1, 3, 5).reshape(NEXP, 8, 128, 4096)
    sgr = sg.reshape(16, 128, 16, 128)
    sur = su.reshape(16, 128, 16, 128)
    for hh in range(2):
        blk = np.stack([sgr[:, :, 8 * hh:8 * hh + 8, :],
                        sur[:, :, 8 * hh:8 * hh + 8, :]], axis=3)
        w1x[NEXP + hh] = blk.transpose(2, 1, 0, 3, 4).reshape(8, 128, 4096)
    w2x = np.empty((NXP, 8, 128, 2048), dtype=np.float32)
    w2x[:NEXP] = w2.reshape(NEXP, 8, 128, 2048)
    sdr = sd.reshape(16, 128, 2048)
    w2x[NEXP] = sdr[:8]
    w2x[NEXP + 1] = sdr[8:]

    j = np.arange(64, dtype=np.float32)
    thetas = np.float32(10000.0) ** (np.float32(-2.0) * j / np.float32(128.0))
    m_all = np.arange(1, T + 1, dtype=np.float32)
    ang = (m_all[None, :] * thetas[:, None]).astype(np.float32)
    cosf = np.concatenate([np.cos(ang), np.cos(ang)], 0).astype(np.float32)
    sinf = np.concatenate([np.sin(ang), np.sin(ang)], 0).astype(np.float32)

    shared = {
        "wq_h": head_stack(Wq), "wk_h": head_stack(Wk),
        "wv_t": f_tiles(Wv), "wo_t": f_tiles(Wo),
        "bq_d": f(bq.reshape(NH, 128).T), "bk_d": f(bk.reshape(NH, 128).T),
        "bv_d": f(bv.reshape(NH, 128).T), "bo_row": f(bo.reshape(1, E)),
        "nw_d": f(nw.reshape(16, 128).T),
        "rwt": f(rw.reshape(NEXP, 16, 128).transpose(1, 2, 0)),
        "w1x": w1x, "w2x": w2x, "ckc": cosf, "cks": sinf,
    }
    in_maps = []
    for c in range(8):
        b, p = c // 2, c % 2
        rows = own_rows(p)
        xb = x[b]
        m = dict(shared)
        m["xb_fm"] = f(xb.T)
        m["xo_fm"] = f(xb[rows].T)
        m["xo_tm"] = f(xb[rows])
        m["cqc"] = f(cosf[:, rows])
        m["cqs"] = f(sinf[:, rows])
        mask = np.zeros((16, 128, 512), dtype=np.float32)
        qpos = np.asarray(rows)
        for s in range(16):
            g = 0 if s < 8 else 1
            kglob = 128 * s + np.arange(128)
            qglob = qpos[512 * g: 512 * g + 512]
            mask[s] = np.where(kglob[:, None] <= qglob[None, :], 0.0, NEG)
        m["maskst"] = mask.astype(ml_dtypes.bfloat16)
        in_maps.append(m)
    return in_maps


gLGT, gRWT, gBRW1, gBRW2 = 14336, 15360, 10240, 12288


def build_program(taps=(), phase_stop=99):
    nc = bass.Bass("TRN2", target_bir_lowering=False)

    def din(name, shape, dtype=F32):
        return nc.dram_tensor(name, shape, dtype, kind="ExternalInput")

    xb_fm = din("xb_fm", [E, T]); xo_fm = din("xo_fm", [E, R]); xo_tm = din("xo_tm", [R, E])
    wq_h = din("wq_h", [NH, 128, 2048]); wk_h = din("wk_h", [NH, 128, 2048])
    wv_t = din("wv_t", [4, 16, 128, 512]); wo_t = din("wo_t", [4, 16, 128, 512])
    bq_d = din("bq_d", [128, 16]); bk_d = din("bk_d", [128, 16]); bv_d = din("bv_d", [128, 16])
    bo_row_d = din("bo_row", [1, E])
    nw_d = din("nw_d", [128, 16]); rwt_d = din("rwt", [16, 128, 32])
    w1x_d = din("w1x", [NXP, 8, 128, 4096]); w2x_d = din("w2x", [NXP, 8, 128, 2048])
    ckc_d = din("ckc", [128, T]); cks_d = din("cks", [128, T])
    cqc_d = din("cqc", [128, R]); cqs_d = din("cqs", [128, R])
    mask_d = din("maskst", [16, 128, 512], BF16)

    oo = nc.dram_tensor("oo", [8, 128, 2048], F32, kind="ExternalOutput")
    q_dram = nc.dram_tensor("q_dram", [NH, 128, R], BF16, kind="Internal")
    k_dram = nc.dram_tensor("k_dram", [NH, 128, T], BF16, kind="Internal")
    v_dram = nc.dram_tensor("v_dram", [4, 16, 128, 512], BF16, kind="Internal")
    s_dram = nc.dram_tensor("s_dram", [8, 128, 2048], F32, kind="Internal")

    tap_t = {}
    for tname, tshape, tdt in taps:
        tap_t[tname] = nc.dram_tensor("tap_" + tname, tshape, tdt, kind="ExternalOutput")

    ident_c = nc.inline_tensor(np.eye(128, dtype=np.float32), "identc")
    rotm = np.zeros((128, 128), dtype=np.float32)
    for m_ in range(64):
        rotm[m_ + 64, m_] = -1.0
        rotm[m_, m_ + 64] = 1.0
    rot_c = nc.inline_tensor(rotm, "rotc")

    al = nc.alloc_sbuf_tensor
    ones_f = al("ones_f", [128, 8], F32)
    ones_b = al("ones_b", [128, 8], BF16)
    onesr_f = al("onesr_f", [1, 128], F32)
    bqs = al("bqs", [128, 16], F32); bks = al("bks", [128, 16], F32); bvs = al("bvs", [128, 16], F32)
    nws = al("nws", [128, 16], F32)
    stats = al("stats", [128, 32], F32)
    gates = al("gates", [128, 8 * NXP], F32)
    rms_b = al("rms_b", [128, R], F32)
    xnw = al("xnw", [128, 16 * R], BF16)
    RD = al("RD", [128, 16384], BF16)
    RE = al("RE", [128, 12288], BF16)
    RF = al("RF", [128, 16384], BF16)
    G = al("G", [128, 16384], F32)
    H = al("H", [128, 4096], F32)
    rot_s = al("rot_s", [128, 128], F32)
    id_s = al("id_s", [128, 128], F32)
    cst = al("cst", [1, 4], F32)
    assert nc.sbuf_bytes_remaining >= 0, nc.sbuf_bytes_remaining

    ctx = ExitStack()
    sems = {}
    SEMNAMES = ["mmt", "vva", "ssa", "gga",
                "dl0", "dl1", "dg0", "dg1", "dgc", "dsl",
                "ds0", "ds1", "dm0", "dm1", "dw0", "dw1", "dst"]
    for sn in SEMNAMES:
        sems[sn] = ctx.enter_context(nc.semaphore(sn))
    cnt = {k: 0 for k in sems}

    TENr, VECr, SCAr, SYNr, GPSr = nc.tensor, nc.vector, nc.scalar, nc.sync, nc.gpsimd

    class Fenced:
        def __init__(self, eng, sn):
            object.__setattr__(self, "eng", eng)
            object.__setattr__(self, "sn", sn)
        def __getattr__(self, name):
            eng, sn = self.eng, self.sn
            fn = getattr(eng, name)
            def call(*a, **kw):
                eng.wait_ge(sems[sn], cnt[sn])
                ins = fn(*a, **kw)
                ins.then_inc(sems[sn], 1)
                cnt[sn] += 1
                return cnt[sn]
            return call
        def wait(self, sn2, val=None):
            self.eng.wait_ge(sems[sn2], cnt[sn2] if val is None else val)

    V = Fenced(VECr, "vva")
    S = Fenced(SCAr, "ssa")

    def tinc(ins):
        ins.then_inc(sems["mmt"], 1)
        cnt["mmt"] += 1
        return cnt["mmt"]

    def twait(sn, val=None):
        TENr.wait_ge(sems[sn], cnt[sn] if val is None else val)

    def dma(queue, sn, out, in_):
        ins = queue.dma_start(out=out, in_=in_)
        ins.then_inc(sems[sn], 16)
        cnt[sn] += 16
        return cnt[sn]

    def qwait(queue, sn, val=None):
        queue.wait_ge(sems[sn], cnt[sn] if val is None else val)

    def row(off, n):
        return G[0:1, off:off + n]

    def gcol(off, n):
        return G[:, off:off + n]

    def finish():
        qwait(SYNr, "dst"); qwait(SYNr, "ds0"); qwait(SYNr, "ds1")
        ctx.close()
        return nc

    for dst_, src_ in ((bqs[:], bq_d.ap()), (bks[:], bk_d.ap()), (bvs[:], bv_d.ap()),
                       (nws[:], nw_d.ap()), (id_s[:], ident_c.ap()), (rot_s[:], rot_c.ap())):
        dma(SYNr, "dsl", dst_, src_)
    dma(SYNr, "dsl", gcol(gRWT, 512).rearrange("p (k c) -> p k c", k=16),
        rwt_d.ap().rearrange("k p c -> p k c"))
    V.memset(ones_f[:], 1.0)
    V.memset(ones_b[:], 1.0)
    V.memset(onesr_f[:], 1.0)
    V.memset(cst[0:1, 0:1], 1.0e-5)
    V.memset(cst[0:1, 1:2], 1.0e-6)
    V.wait("dsl")
    for k in range(16):
        sl = gcol(gRWT + 32 * k, 32)
        V.tensor_scalar(sl, sl, nws[:, k:k + 1], None, ALU.mult)

    XB = (0, 1024); SQ = (2048, 3072)
    rMU, rVAR, rRSTD, rNEGM, rRMS, rTMP = 4096, 5120, 6144, 7168, 8192, 9216
    BC0, BC1 = 4096, 5120
    p1a = ExitStack()
    with nc.psum_tensor("ps_lgt", [32, 1024], F32) as ps_lgt:
        ps_sum = p1a.enter_context(nc.psum_tensor("ps_sum", [1, 1024], F32))
        ps_sq = p1a.enter_context(nc.psum_tensor("ps_sq", [1, 1024], F32))

        used1, sq1, ld1 = [], [], []
        for k in range(16):
            xb, sq = XB[k % 2], SQ[k % 2]
            par = "dl0" if k % 2 == 0 else "dg1"
            qq = SYNr if k % 2 == 0 else GPSr
            if k >= 2:
                qwait(qq, "mmt", used1[k - 2])
            ld1.append(dma(qq, par, gcol(xb, 1024), xo_fm[128 * k:128 * (k + 1), :]))
            S.wait(par, ld1[k])
            if k >= 2:
                S.wait("mmt", used1[k - 2])
            sq1.append(S.activation(gcol(sq, 1024), gcol(xb, 1024), AF.Square))
            twait("ssa", sq1[k])
            if k == 0:
                twait("vva")
            for h2 in range(2):
                TENr.matmul(ps_sum[:, 512 * h2:512 * (h2 + 1)], ones_f[:, 0:1],
                            gcol(xb + 512 * h2, 512), start=(k == 0), stop=(k == 15))
                ins = TENr.matmul(ps_sq[:, 512 * h2:512 * (h2 + 1)], ones_f[:, 0:1],
                                  gcol(sq + 512 * h2, 512), start=(k == 0), stop=(k == 15))
            used1.append(tinc(ins))

        V.wait("mmt")
        V.tensor_scalar(row(rMU, 1024), ps_sum[0:1, :], 1.0 / 2048.0, None, ALU.mult)
        V.tensor_scalar(row(rVAR, 1024), ps_sq[0:1, :], 1.0 / 2048.0, None, ALU.mult)
        V.tensor_tensor(row(rTMP, 1024), row(rMU, 1024), row(rMU, 1024), ALU.mult)
        V.tensor_tensor(row(rVAR, 1024), row(rVAR, 1024), row(rTMP, 1024), ALU.subtract)
        S.wait("vva")
        S.activation(row(rTMP, 1024), row(rVAR, 1024), AF.Ln, bias=cst[0:1, 0:1])
        S.activation(row(rRSTD, 1024), row(rTMP, 1024), AF.Exp, scale=-0.5)
        V.wait("ssa")
        V.scalar_tensor_tensor(row(rNEGM, 1024), row(rMU, 1024), -1.0, row(rRSTD, 1024), ALU.mult, ALU.mult)
        V.tensor_scalar(row(rTMP, 1024), row(rVAR, 1024), 1.0e-5, None, ALU.add)
        V.reciprocal(row(rMU, 1024), row(rTMP, 1024))
        V.tensor_tensor(row(rVAR, 1024), row(rVAR, 1024), row(rMU, 1024), ALU.mult)
        S.wait("vva")
        S.activation(row(rTMP, 1024), row(rVAR, 1024), AF.Ln, bias=cst[0:1, 1:2])
        S.activation(row(rRMS, 1024), row(rTMP, 1024), AF.Exp, scale=-0.5)
        p1a.close()
        ps_bc = p1a.enter_context(nc.psum_tensor("ps_bc", [128, 1024], F32))
        ps_st = p1a.enter_context(nc.psum_tensor("ps_st", [128, 32], F32))

        twait("ssa")
        for i, roff in enumerate((rRSTD, rNEGM, rRMS)):
            for tc in range(8):
                ins = TENr.matmul(ps_st[:, 8 * i + tc:8 * i + tc + 1], row(roff + 128 * tc, 128),
                                  onesr_f[0:1, 0:1], start=True, stop=True)
        m_st = tinc(ins)
        V.wait("mmt", m_st)
        V.tensor_copy(stats[:, 0:24], ps_st[:, 0:24])

        for src_, dst_ in ((rRMS, rms_b[:, :]), (rRSTD, gcol(BC0, 1024)), (rNEGM, gcol(BC1, 1024))):
            twait("vva")
            for h2 in range(2):
                ins = TENr.matmul(ps_bc[:, 512 * h2:512 * (h2 + 1)], onesr_f[:, :],
                                  row(src_ + 512 * h2, 512), start=True, stop=True)
            m_ = tinc(ins)
            V.wait("mmt", m_)
            V.tensor_copy(dst_, ps_bc[:, :])

        used2, ld2, xn2, vv2 = [], [], [], []
        qwait(SYNr, "ssa"); qwait(SYNr, "mmt"); qwait(SYNr, "vva")
        qwait(GPSr, "ssa"); qwait(GPSr, "mmt"); qwait(GPSr, "vva")
        for k in range(16):
            xb, sc = XB[k % 2], SQ[k % 2]
            par = "dl0" if k % 2 == 0 else "dg1"
            qq = SYNr if k % 2 == 0 else GPSr
            if k >= 2:
                qwait(qq, "vva", vv2[k - 2])
            ld2.append(dma(qq, par, gcol(xb, 1024), xo_fm[128 * k:128 * (k + 1), :]))
            V.wait(par, ld2[k])
            if k >= 2:
                V.wait("mmt", used2[k - 2])
                V.wait("ssa", xn2[k - 2])
            c_tt = V.tensor_tensor(gcol(sc, 1024), gcol(xb, 1024), gcol(BC0, 1024), ALU.mult)
            vv2.append(c_tt)
            V.tensor_tensor(gcol(sc, 1024), gcol(sc, 1024), gcol(BC1, 1024), ALU.add)
            c_xnw = V.tensor_scalar(xnw[:, R * k:R * (k + 1)], gcol(sc, 1024), nws[:, k:k + 1], None, ALU.mult)
            twait("vva", c_xnw)
            TENr.matmul(ps_lgt[:, 0:512], gcol(gRWT + 32 * k, 32), gcol(sc, 512), start=(k == 0), stop=(k == 15))
            ins = TENr.matmul(ps_lgt[:, 512:1024], gcol(gRWT + 32 * k, 32), gcol(sc + 512, 512), start=(k == 0), stop=(k == 15))
            used2.append(tinc(ins))
            S.wait("vva", c_xnw)
            xn2.append(S.copy(RD[:, R * k:R * (k + 1)], gcol(sc, 1024)))

        V.wait("mmt")
        V.tensor_copy(G[0:32, gLGT:gLGT + 1024], ps_lgt[:, :])
        p1a.close()

    def tap(name, src_ap):
        if name in tap_t:
            for sn_ in ("vva", "ssa", "mmt", "gga", "ds0", "ds1", "dgc", "dst"):
                qwait(SYNr, sn_)
            dma(SYNr, "dst", tap_t[name].ap(), src_ap)

    tap("xnw", xnw[:, :])
    tap("xn_own", RD[:, 0:16 * R])
    tap("lgt", G[0:32, gLGT:gLGT + 1024])
    tap("stats", stats[:, 0:24])
    tap("rmsb", rms_b[:, :])
    if phase_stop <= 1:
        return finish()

    bMU, bVAR, bTMP = 0, 2048, 4096
    with nc.psum_tensor("ps_sum2", [1, 2048], F32) as ps_sum2, \
         nc.psum_tensor("ps_sq2", [1, 2048], F32) as ps_sq2:
        usedb, sqb, ldb = [], [], []
        for k in range(16):
            xb, sq = 2048 * (k % 2), 4096 + 2048 * (k % 2)
            par = "dg0" if k % 2 == 0 else "dg1"
            if k >= 2:
                qwait(GPSr, "mmt", usedb[k - 2])
            ldb.append(dma(GPSr, par, RE[:, xb:xb + 2048], xb_fm[128 * k:128 * (k + 1), :]))
            S.wait(par, ldb[k])
            if k >= 2:
                S.wait("mmt", usedb[k - 2])
            sqb.append(S.activation(RE[:, sq:sq + 2048], RE[:, xb:xb + 2048], AF.Square))
            twait("ssa", sqb[k])
            if k == 0:
                twait("vva")
            for h4 in range(4):
                TENr.matmul(ps_sum2[:, 512 * h4:512 * (h4 + 1)], ones_b[:, 0:1],
                            RE[:, xb + 512 * h4:xb + 512 * (h4 + 1)], start=(k == 0), stop=(k == 15))
                ins = TENr.matmul(ps_sq2[:, 512 * h4:512 * (h4 + 1)], ones_b[:, 0:1],
                                  RE[:, sq + 512 * h4:sq + 512 * (h4 + 1)], start=(k == 0), stop=(k == 15))
            usedb.append(tinc(ins))

        V.wait("mmt")
        V.tensor_scalar(row(bMU, 2048), ps_sum2[0:1, :], 1.0 / 2048.0, None, ALU.mult)
        V.tensor_scalar(row(bVAR, 2048), ps_sq2[0:1, :], 1.0 / 2048.0, None, ALU.mult)
        V.tensor_tensor(row(bTMP, 2048), row(bMU, 2048), row(bMU, 2048), ALU.mult)
        V.tensor_tensor(row(bVAR, 2048), row(bVAR, 2048), row(bTMP, 2048), ALU.subtract)
        S.wait("vva")
        S.activation(row(bTMP, 2048), row(bVAR, 2048), AF.Ln, bias=cst[0:1, 0:1])
        S.activation(row(gBRW1, 2048), row(bTMP, 2048), AF.Exp, scale=-0.5)
        V.wait("ssa")
        V.scalar_tensor_tensor(row(gBRW2, 2048), row(bMU, 2048), -1.0, row(gBRW1, 2048), ALU.mult, ALU.mult)

    tap("brw1", row(gBRW1, 2048))
    tap("brw2", row(gBRW2, 2048))
    if phase_stop <= 2:
        return finish()

    lgS, swS, scS, gnS = 0, 32, 64, 128
    mxS, m4S, dnS = 96, 97, 98
    with nc.psum_tensor("ps_t7", [128, 32], F32) as ps_t7:
        for tc in range(8):
            twait("vva")
            ins = TENr.transpose(ps_t7[:, 0:32], G[0:32, gLGT + 128 * tc:gLGT + 128 * (tc + 1)], id_s[0:32, 0:32])
            m_ = tinc(ins)
            V.wait("mmt", m_)
            V.tensor_copy(gcol(lgS, 32), ps_t7[:, 0:32])
            V.tensor_scalar(gcol(lgS, 32), gcol(lgS, 32), stats[:, 16 + tc:17 + tc], None, ALU.mult)
            V.reduce_max(gcol(mxS, 1), gcol(lgS, 32), axis=AX.X, negate=True)
            S.wait("vva")
            S.activation(gcol(scS, 32), gcol(lgS, 32), AF.Exp, bias=gcol(mxS, 1))
            V.wait("ssa")
            V.tensor_copy(gcol(swS, 32), gcol(scS, 32))
            for r_ in range(3):
                V.reduce_max(gcol(m4S, 1), gcol(swS, 32), axis=AX.X)
                V.tensor_scalar(gcol(gnS, 32), gcol(swS, 32), gcol(m4S, 1), None, ALU.is_ge)
                V.scalar_tensor_tensor(gcol(swS, 32), gcol(gnS, 32), NEG, gcol(swS, 32), ALU.mult, ALU.add)
            V.reduce_max(gcol(m4S, 1), gcol(swS, 32), axis=AX.X)
            V.tensor_scalar(gcol(gnS, 32), gcol(scS, 32), gcol(m4S, 1), None, ALU.is_ge)
            V.tensor_tensor(gcol(gnS, 32), gcol(scS, 32), gcol(gnS, 32), ALU.mult)
            V.reduce_sum(gcol(dnS, 1), gcol(gnS, 32), axis=AX.X)
            V.reciprocal(gcol(m4S, 1), gcol(dnS, 1))
            V.tensor_scalar(gcol(gnS, 32), gcol(gnS, 32), gcol(m4S, 1), None, ALU.mult)
            V.tensor_scalar(gates[:, NXP * tc:NXP * tc + 32], gcol(gnS, 32), stats[:, 16 + tc:17 + tc], None, ALU.mult)
            V.tensor_copy(gates[:, NXP * tc + 32:NXP * tc + 33], stats[:, 16 + tc:17 + tc])
            V.tensor_copy(gates[:, NXP * tc + 33:NXP * tc + 34], stats[:, 16 + tc:17 + tc])

    tap("gates", gates[:, :])
    if phase_stop <= 3:
        return finish()

    cqcS, cqsS, qfS, t1S, t2S = 0, 1024, 2048, 4096, 5120
    qwait(SYNr, "vva"); qwait(SYNr, "ssa")
    dma(SYNr, "dsl", gcol(cqcS, 1024), cqc_d.ap())
    dma(SYNr, "dsl", gcol(cqsS, 1024), cqs_d.ap())
    with nc.psum_tensor("ps_q", [128, 2 * 1024], F32) as ps_q, \
         nc.psum_tensor("ps_r", [128, 2 * 1024], F32) as ps_r:
        qmm, qev_bias, qev_rd, qev_q, qst = [], [], [], [], []
        qwait(GPSr, "mmt"); qwait(GPSr, "ssa"); qwait(GPSr, "vva")
        qwait(SYNr, "mmt")

        def q_stageA(h):
            par = "dg0" if h % 2 == 0 else "dg1"
            ofs = 2048 * (h % 2)
            if h >= 2:
                qwait(GPSr, "mmt", qmm[h - 2])
            wql = dma(GPSr, par, RE[:, ofs:ofs + 2048], wq_h[h])
            twait(par, wql)
            if h == 0:
                twait("ssa"); twait("dsl"); twait("vva")
            if h >= 2:
                twait("vva", qev_bias[h - 2])
            for k in range(16):
                for h2 in range(2):
                    ins = TENr.matmul(ps_q[:, 1024 * (h % 2) + 512 * h2:1024 * (h % 2) + 512 * (h2 + 1)],
                                      RE[:, ofs + 128 * k:ofs + 128 * (k + 1)],
                                      RD[:, R * k + 512 * h2:R * k + 512 * (h2 + 1)],
                                      start=(k == 0), stop=(k == 15))
            qmm.append(tinc(ins))
            V.wait("mmt", qmm[h])
            if h >= 2:
                V.wait("mmt", qrot_mm[h - 2] if len(qrot_mm) > h - 2 else 0)
            qev_bias.append(V.tensor_scalar(gcol(qfS + 1024 * (h % 2), 1024),
                                            ps_q[:, 1024 * (h % 2):1024 * (h % 2) + 1024],
                                            bqs[:, h:h + 1], None, ALU.add))

        qrot_mm = []

        def q_stageB(h):
            twait("vva", qev_bias[h])
            if h >= 2:
                twait("vva", qev_rd[h - 2])
            for h2 in range(2):
                ins = TENr.matmul(ps_r[:, 1024 * (h % 2) + 512 * h2:1024 * (h % 2) + 512 * (h2 + 1)],
                                  rot_s[:, :], gcol(qfS + 1024 * (h % 2) + 512 * h2, 512), start=True, stop=True)
            qrot_mm.append(tinc(ins))
            V.wait("mmt", qrot_mm[h])
            V.tensor_tensor(gcol(t1S, 1024), gcol(qfS + 1024 * (h % 2), 1024), gcol(cqcS, 1024), ALU.mult)
            qev_rd.append(V.tensor_tensor(gcol(t2S, 1024), ps_r[:, 1024 * (h % 2):1024 * (h % 2) + 1024],
                                          gcol(cqsS, 1024), ALU.mult))
            spar = "ds0" if h % 2 == 0 else "ds1"
            if h >= 2:
                V.wait(spar, qst[h - 2])
            qev_q.append(V.tensor_tensor(RE[:, 4096 + 1024 * (h % 2):4096 + 1024 * (h % 2) + 1024],
                                         gcol(t1S, 1024), gcol(t2S, 1024), ALU.add))
            qwait(SYNr, "vva", qev_q[h])
            qst.append(dma(SYNr, spar, q_dram[h], RE[:, 4096 + 1024 * (h % 2):4096 + 1024 * (h % 2) + 1024]))

        for h in range(NH):
            q_stageA(h)
            if h >= 1:
                q_stageB(h - 1)
        q_stageB(NH - 1)
    tap("q_dram", q_dram.ap())
    if phase_stop <= 4:
        return finish()

    ckcS, cksS, kfS, t1S4, BCh0, BCh1, t2S4 = 0, 2048, 4096, 6144, 6144, 7168, 8192
    qwait(SYNr, "vva")
    dma(SYNr, "dsl", gcol(ckcS, 2048), ckc_d.ap())
    dma(SYNr, "dsl", gcol(cksS, 2048), cks_d.ap())
    with nc.psum_tensor("ps_k", [128, 2 * 1024], F32) as ps_k, \
         nc.psum_tensor("ps_kr", [128, 1024], F32) as ps_kr, \
         nc.psum_tensor("ps_v", [128, 2 * 512], F32) as ps_v:
        ps_bch = ps_v
        for hh in range(2):
            chs = slice(1024 * hh, 1024 * (hh + 1))
            twait("vva"); twait("ssa")
            for h2 in range(2):
                ins = TENr.matmul(ps_bch[:, 512 * h2:512 * (h2 + 1)], onesr_f[:, :],
                                  row(gBRW1 + 1024 * hh + 512 * h2, 512), start=True, stop=True)
            m_ = tinc(ins)
            V.wait("mmt", m_)
            V.tensor_copy(gcol(BCh0, 1024), ps_bch[:, :])
            twait("vva")
            for h2 in range(2):
                ins = TENr.matmul(ps_bch[:, 512 * h2:512 * (h2 + 1)], onesr_f[:, :],
                                  row(gBRW2 + 1024 * hh + 512 * h2, 512), start=True, stop=True)
            m_ = tinc(ins)
            V.wait("mmt", m_)
            V.tensor_copy(gcol(BCh1, 1024), ps_bch[:, :])
            xnb, xbld = [], []
            qwait(GPSr, "vva"); qwait(GPSr, "ssa"); qwait(GPSr, "mmt")
            qwait(SYNr, "vva"); qwait(SYNr, "ssa"); qwait(SYNr, "mmt")
            V.wait("mmt")
            for k in range(16):
                if k % 2 == 0:
                    par, ofs = "dg0", 0
                    if k >= 2:
                        qwait(GPSr, "vva", xnb[k - 2])
                    xbld.append(dma(GPSr, par, RE[:, ofs:ofs + 1024], xb_fm[128 * k:128 * (k + 1), chs]))
                    src = RE[:, ofs:ofs + 1024]
                else:
                    par = "dm1"
                    sofs = 14336 + 1024 * ((k // 2) % 2)
                    if k >= 5:
                        qwait(SYNr, "vva", xnb[k - 4])
                    xbld.append(dma(SYNr, par, gcol(sofs, 1024), xb_fm[128 * k:128 * (k + 1), chs]))
                    src = gcol(sofs, 1024)
                V.wait(par, xbld[k])
                V.tensor_tensor(gcol(t2S4, 1024), src, gcol(BCh0, 1024), ALU.mult)
                xnb.append(V.tensor_tensor(RD[:, 1024 * k:1024 * (k + 1)], gcol(t2S4, 1024), gcol(BCh1, 1024), ALU.add))
            kmm, kev_bias, kev_rd, kev_k, kst, krot_mm = [], [], [], [], [], []
            V.wait("ds0"); V.wait("ds1")
            qwait(GPSr, "vva")

            def k_stageA(h):
                par = "dg0" if h % 2 == 0 else "dg1"
                ofs = 2048 * (h % 2)
                if h >= 2:
                    qwait(GPSr, "mmt", kmm[h - 2])
                wkl = dma(GPSr, par, RE[:, ofs:ofs + 2048], wk_h[h])
                twait(par, wkl)
                if h == 0:
                    twait("vva"); twait("dsl")
                if h >= 2:
                    twait("vva", kev_bias[h - 2])
                for k in range(16):
                    for h2 in range(2):
                        ins = TENr.matmul(ps_k[:, 1024 * (h % 2) + 512 * h2:1024 * (h % 2) + 512 * (h2 + 1)],
                                          RE[:, ofs + 128 * k:ofs + 128 * (k + 1)],
                                          RD[:, 1024 * k + 512 * h2:1024 * k + 512 * (h2 + 1)],
                                          start=(k == 0), stop=(k == 15))
                kmm.append(tinc(ins))
                V.wait("mmt", kmm[h])
                if h >= 2:
                    V.wait("mmt", krot_mm[h - 2])
                kev_bias.append(V.tensor_scalar(gcol(kfS + 1024 * (h % 2), 1024),
                                                ps_k[:, 1024 * (h % 2):1024 * (h % 2) + 1024],
                                                bks[:, h:h + 1], None, ALU.add))

            def k_stageB(h):
                twait("vva", kev_bias[h])
                if h >= 1:
                    twait("vva", kev_rd[h - 1])
                for h2 in range(2):
                    ins = TENr.matmul(ps_kr[:, 512 * h2:512 * (h2 + 1)],
                                      rot_s[:, :], gcol(kfS + 1024 * (h % 2) + 512 * h2, 512), start=True, stop=True)
                krot_mm.append(tinc(ins))
                V.wait("mmt", krot_mm[h])
                V.tensor_tensor(gcol(t1S4, 1024), gcol(kfS + 1024 * (h % 2), 1024), gcol(ckcS + 1024 * hh, 1024), ALU.mult)
                kev_rd.append(V.tensor_tensor(gcol(t2S4, 1024), ps_kr[:, 0:1024],
                                              gcol(cksS + 1024 * hh, 1024), ALU.mult))
                spar = "ds0" if h % 2 == 0 else "ds1"
                if h >= 2:
                    V.wait(spar, kst[h - 2])
                kev_k.append(V.tensor_tensor(RE[:, 4096 + 1024 * (h % 2):4096 + 1024 * (h % 2) + 1024],
                                             gcol(t1S4, 1024), gcol(t2S4, 1024), ALU.add))
                qwait(SYNr, "vva", kev_k[h])
                kst.append(dma(SYNr, spar, k_dram[h][:, chs],
                               RE[:, 4096 + 1024 * (h % 2):4096 + 1024 * (h % 2) + 1024]))

            for h in range(NH):
                k_stageA(h)
                if h >= 1:
                    k_stageB(h - 1)
            k_stageB(NH - 1)
            vst = []
            vmm = []
            for ft in range(4):
                ofsF = 8192 * (ft % 2)
                if ft >= 2:
                    qwait(GPSr, "mmt", vmm[ft - 2])
                if ft == 0 and hh == 1:
                    qwait(GPSr, "mmt")
                wvl = dma(GPSr, "dgc", RF[:, ofsF:ofsF + 8192].rearrange("p (k c) -> p k c", k=16),
                          wv_t[ft].rearrange("k p c -> p k c"))
                twait("dgc", wvl)
                if ft == 0:
                    twait("vva")
                for tk in range(8):
                    pv = tk % 2
                    nn = len(vst)
                    if nn >= 2:
                        twait("vva", vst[nn - 2][0])
                    for k in range(16):
                        ins = TENr.matmul(ps_v[:, 512 * pv:512 * (pv + 1)],
                                          RD[:, 1024 * k + 128 * tk:1024 * k + 128 * (tk + 1)],
                                          RF[:, ofsF + 512 * k:ofsF + 512 * (k + 1)],
                                          start=(k == 0), stop=(k == 15))
                    m_v = tinc(ins)
                    V.wait("mmt", m_v)
                    spar = "ds0" if nn % 2 == 0 else "ds1"
                    if nn >= 2:
                        V.wait(spar, vst[nn - 2][1])
                    c_v = V.tensor_copy(RE[:, 8192 + 512 * pv:8192 + 512 * (pv + 1)], ps_v[:, 512 * pv:512 * (pv + 1)])
                    qwait(SYNr, "vva", c_v)
                    st_ = dma(SYNr, spar, v_dram[ft, 8 * hh + tk], RE[:, 8192 + 512 * pv:8192 + 512 * (pv + 1)])
                    vst.append((c_v, st_))
                vmm.append(cnt["mmt"])
    tap("k_dram", k_dram.ap())
    tap("v_dram", v_dram.ap())
    if phase_stop <= 5:
        return finish()

    PTo, KHo, QHo, MKo = 0, 8192, 12288, 14336
    linvS, lbS, yscrS = 0, 1024, 2048
    with nc.psum_tensor("ps_s", [128, 2 * 1024], F32) as ps_s, \
         nc.psum_tensor("ps_l2", [128, 1024], F32) as ps_l2, \
         nc.psum_tensor("ps_y", [128, 1024], F32) as ps_y:
        qwait(SYNr, "vva"); qwait(SYNr, "ssa"); qwait(SYNr, "mmt")
        qwait(SYNr, "ds0"); qwait(SYNr, "ds1")
        qwait(GPSr, "ds0"); qwait(GPSr, "ds1")
        khld, qhld, st_done, yevs, lrec, lbcp = [], [], [], [], [], []
        exps_pairs = []
        vgld = [None] * 4
        mskld, mskcons = [], []
        for h in range(NH):
            par = "dl0" if h % 2 == 0 else "dl1"
            kofs = KHo + 2048 * (h % 2)
            qofs = QHo + 1024 * (h % 2)
            if h >= 2:
                qwait(SYNr, "mmt", st_done[h - 2])
            khld.append(dma(SYNr, par, RD[:, kofs:kofs + 2048], k_dram[h]))
            qhld.append(dma(SYNr, par, RD[:, qofs:qofs + 1024], q_dram[h]))
            if h % 4 == 0:
                if h == 0:
                    qwait(GPSr, "mmt")
                else:
                    qwait(GPSr, "mmt", st_done[h - 1])
                for tkv in range(16):
                    vg_m = dma(GPSr, "dgc", RE[:, 512 * tkv:512 * (tkv + 1)], v_dram[h // 4, tkv])
                vgld[h // 4] = vg_m
            pv_done = None
            for g01 in range(2):
                Eg = EG[g01]
                exp_marks = []
                for pr in range(Eg // 2):
                    jt0 = 2 * pr
                    slot = len(exps_pairs) % 2
                    twait(par, qhld[h])
                    if pr == 0 and g01 == 0 and h == 0:
                        twait("vva"); twait("ssa")
                    if pr == 0 and g01 == 0 and h % 4 == 0:
                        twait("dgc", vgld[h // 4])
                    if len(exps_pairs) >= 2:
                        twait("ssa", exps_pairs[-2])
                    for dj in range(2):
                        ins = TENr.matmul(ps_s[:, 1024 * slot + 512 * dj:1024 * slot + 512 * (dj + 1)],
                                          RD[:, kofs + 128 * (jt0 + dj):kofs + 128 * (jt0 + dj + 1)],
                                          RD[:, qofs + 512 * g01:qofs + 512 * (g01 + 1)],
                                          start=True, stop=True)
                    m_s = tinc(ins)
                    masked = (g01 == 0) or (jt0 >= 8)
                    if g01 == 1 and pr == 0:
                        S.wait("mmt", pv_done)
                    if g01 == 0 and pr == 0 and h >= 1:
                        S.wait("mmt", st_done[h - 1])
                    if masked:
                        mpar = "dm0" if len(mskld) % 2 == 0 else "dm1"
                        if len(mskld) >= 2:
                            qwait(SYNr, "vva", mskcons[len(mskld) - 2])
                        mof = MKo + 1024 * (len(mskld) % 2)
                        mskld.append(dma(SYNr, mpar,
                                         RD[:, mof:mof + 1024].rearrange("p (s c) -> p s c", s=2),
                                         mask_d.ap()[jt0:jt0 + 2].rearrange("s p c -> p s c")))
                        V.wait(mpar, mskld[-1])
                        V.wait("mmt", m_s)
                        c_m = V.tensor_tensor(ps_s[:, 1024 * slot:1024 * (slot + 1)],
                                              ps_s[:, 1024 * slot:1024 * (slot + 1)],
                                              RD[:, mof:mof + 1024], ALU.add)
                        mskcons.append(c_m)
                        S.wait("vva", c_m)
                    else:
                        S.wait("mmt", m_s)
                    em = S.activation(RD[:, PTo + 512 * jt0:PTo + 512 * (jt0 + 2)],
                                      ps_s[:, 1024 * slot:1024 * (slot + 1)], AF.Exp, scale=ISQD)
                    exp_marks.append(em)
                    exps_pairs.append(em)
                if g01 == 0:
                    if h >= 1:
                        twait("vva", lrec[h - 1])
                        twait("vva", yevs[h - 1])
                        twait("vva", lbcp[h - 1])
                for jt in range(Eg):
                    twait("ssa", exp_marks[jt // 2])
                    TENr.matmul(ps_l2[0:1, 512 * g01:512 * (g01 + 1)], ones_b[:, 0:1],
                                RD[:, PTo + 512 * jt:PTo + 512 * (jt + 1)],
                                start=(jt == 0), stop=(jt == Eg - 1))
                    ins = TENr.matmul(ps_y[:, 512 * g01:512 * (g01 + 1)],
                                      RE[:, 512 * jt + 128 * (h % 4):512 * jt + 128 * (h % 4) + 128],
                                      RD[:, PTo + 512 * jt:PTo + 512 * (jt + 1)],
                                      start=(jt == 0), stop=(jt == Eg - 1))
                pv_done = tinc(ins)
            st_done.append(pv_done)
            V.wait("mmt", st_done[h])
            c_li = V.reciprocal(row(linvS, 1024), ps_l2[0:1, :])
            lrec.append(c_li)
            twait("vva", c_li)
            for h2 in range(2):
                ins = TENr.matmul(ps_l2[:, 512 * h2:512 * (h2 + 1)], onesr_f[:, :],
                                  row(linvS + 512 * h2, 512), start=True, stop=True)
            m_lb = tinc(ins)
            V.wait("mmt", m_lb)
            lbcp.append(V.tensor_copy(gcol(lbS, 1024), ps_l2[:, :]))
            V.tensor_tensor(gcol(yscrS, 1024), ps_y[:, :], gcol(lbS, 1024), ALU.mult)
            yevs.append(V.tensor_scalar(RF[:, 1024 * h:1024 * (h + 1)], gcol(yscrS, 1024),
                                        bvs[:, h:h + 1], None, ALU.add))
    tap("y_fm", RF[:, 0:16384])
    if phase_stop <= 6:
        return finish()

    boS, borowS = 0, 2048
    with nc.psum_tensor("ps_o", [128, 2 * 512], F32) as ps_o, \
         nc.psum_tensor("ps_bo", [128, 2048], F32) as ps_bo:
        qwait(SYNr, "vva")
        dma(SYNr, "dsl", row(borowS, 2048), bo_row_d.ap())
        twait("dsl"); twait("vva")
        for q4 in range(4):
            ins = TENr.matmul(ps_bo[:, 512 * q4:512 * (q4 + 1)], onesr_f[:, :],
                              row(borowS + 512 * q4, 512), start=True, stop=True)
        m_bo = tinc(ins)
        V.wait("mmt", m_bo)
        V.tensor_copy(gcol(boS, 2048), ps_bo[:, :])
        ost = []
        womm = []
        for ft in range(4):
            ofsF = 8192 * (ft % 2)
            if ft >= 2:
                qwait(GPSr, "mmt", womm[ft - 2])
            if ft == 0:
                qwait(GPSr, "mmt"); qwait(GPSr, "vva")
            wol = dma(GPSr, "dgc", RD[:, ofsF:ofsF + 8192].rearrange("p (k c) -> p k c", k=16),
                      wo_t[ft].rearrange("k p c -> p k c"))
            twait("dgc", wol)
            for tc in range(8):
                pv = tc % 2
                nn = len(ost)
                if nn >= 2:
                    twait("vva", ost[nn - 2][0])
                for k in range(16):
                    ins = TENr.matmul(ps_o[:, 512 * pv:512 * (pv + 1)],
                                      RF[:, 1024 * k + 128 * tc:1024 * k + 128 * (tc + 1)],
                                      RD[:, ofsF + 512 * k:ofsF + 512 * (k + 1)],
                                      start=(k == 0), stop=(k == 15))
                m_o = tinc(ins)
                V.wait("mmt", m_o)
                spar = "ds0" if nn % 2 == 0 else "ds1"
                if nn >= 2:
                    V.wait(spar, ost[nn - 2][1])
                c_o = V.tensor_tensor(H[:, 512 * pv:512 * (pv + 1)], ps_o[:, 512 * pv:512 * (pv + 1)],
                                      gcol(boS + 512 * ft, 512), ALU.add)
                qwait(SYNr, "vva", c_o)
                st_ = dma(SYNr, spar, s_dram[tc][:, 512 * ft:512 * (ft + 1)], H[:, 512 * pv:512 * (pv + 1)])
                ost.append((c_o, st_))
            womm.append(cnt["mmt"])
    tap("s_dram", s_dram.ap())
    if phase_stop <= 7:
        return finish()

    with nc.psum_tensor("ps_g", [128, 2 * 512], F32) as ps_g, \
         nc.psum_tensor("ps_u", [128, 2 * 512], F32) as ps_u, \
         nc.psum_tensor("ps_w2", [128, 2 * 512], F32) as ps_w2:
        V.wait("mmt"); V.wait("ssa"); V.wait("ds0"); V.wait("ds1"); V.wait("dst")
        V.memset(G[:, 0:8192], 0.0)
        V.memset(G[:, 8192:16384], 0.0)
        for tc in range(8):
            V.tensor_scalar(G[:, 2048 * tc:2048 * (tc + 1)], G[:, 2048 * tc:2048 * (tc + 1)],
                            stats[:, 8 + tc:9 + tc], None, ALU.add)
        w1mm = {}
        w1ld = []
        evac = {}
        w2mm = [0] * NXP
        w2fhmm = {}
        sttd = []
        qwait(GPSr, "vva"); qwait(GPSr, "ssa"); qwait(GPSr, "mmt")
        for e in range(NXP):
            for jj in range(8):
                gi = 8 * e + jj
                par = "dg0" if gi % 2 == 0 else "dg1"
                ofs = 8192 + 4096 * (gi % 2)
                if gi >= 2:
                    qwait(GPSr, "mmt", w1mm[(gi - 2, 1)])
                w1ld.append(dma(GPSr, par, RD[:, ofs:ofs + 4096], w1x_d[e, jj]))
                twait(par, w1ld[gi])
                for th in range(2):
                    if gi == 0 and th == 0:
                        twait("vva"); twait("ssa")
                    if gi >= 1:
                        twait("vva", evac[(gi - 1, th)])
                    if jj == 0 and th == 0 and e >= 1:
                        twait("mmt", w2mm[e - 1])
                    for half, ps in ((0, ps_g), (1, ps_u)):
                        for k in range(16):
                            ins = TENr.matmul(ps[:, 512 * th:512 * (th + 1)],
                                              RD[:, ofs + 256 * k + 128 * half:ofs + 256 * k + 128 * (half + 1)],
                                              xnw[:, R * k + 512 * th:R * k + 512 * (th + 1)],
                                              start=(k == 0), stop=(k == 15))
                    w1mm[(gi, th)] = tinc(ins)
                    V.wait("mmt", w1mm[(gi, th)])
                    V.tensor_tensor(H[:, 0:512], ps_g[:, 512 * th:512 * (th + 1)],
                                    rms_b[:, 512 * th:512 * (th + 1)], ALU.mult)
                    S.wait("vva")
                    S.activation(H[:, 512:1024], H[:, 0:512], AF.Silu)
                    V.wait("ssa")
                    evac[(gi, th)] = V.tensor_tensor(RD[:, 1024 * jj + 512 * th:1024 * jj + 512 * (th + 1)],
                                                     H[:, 512:1024], ps_u[:, 512 * th:512 * (th + 1)], ALU.mult)
            for fh in range(2):
                gi2 = 2 * e + fh
                ofs2 = 8192 * (gi2 % 2)
                wpar = "dw0" if gi2 % 2 == 0 else "dw1"
                if e >= 1:
                    qwait(GPSr, "mmt", w2fhmm[(e - 1, fh)])
                w2l = dma(GPSr, wpar, RF[:, ofs2:ofs2 + 8192].rearrange("p (i c) -> p i c", i=8),
                          w2x_d[e][:, :, 1024 * fh:1024 * (fh + 1)].rearrange("i p c -> p i c"))
                twait(wpar, w2l)
                twait("vva", evac[(8 * e + 7, 1)])
                for tc in range(8):
                    for f2 in range(2):
                        sidx = len(sttd)
                        pv = sidx % 2
                        if sidx >= 2:
                            twait("vva", sttd[sidx - 2])
                        for i in range(8):
                            ins = TENr.matmul(ps_w2[:, 512 * pv:512 * (pv + 1)],
                                              RD[:, 1024 * i + 128 * tc:1024 * i + 128 * (tc + 1)],
                                              RF[:, ofs2 + 1024 * i + 512 * f2:ofs2 + 1024 * i + 512 * (f2 + 1)],
                                              start=(i == 0), stop=(i == 7))
                        m_w2 = tinc(ins)
                        V.wait("mmt", m_w2)
                        go = 2048 * tc + 1024 * fh + 512 * f2
                        sttd.append(V.scalar_tensor_tensor(gcol(go, 512), ps_w2[:, 512 * pv:512 * (pv + 1)],
                                                           gates[:, NXP * tc + e:NXP * tc + e + 1],
                                                           gcol(go, 512), ALU.mult, ALU.add))
                w2fhmm[(e, fh)] = cnt["mmt"]
            w2mm[e] = cnt["mmt"]
    tap("routed", G[:, :])
    if phase_stop <= 8:
        return finish()

    xld, sld = [], []
    steps4 = []
    qwait(SYNr, "vva", evac[(8 * NXP - 1, 1)])
    qwait(GPSr, "vva", evac[(8 * NXP - 1, 1)])
    for it in range(16):
        tc, hf = it // 2, it % 2
        xb = H[:, 2048 * (it % 2):2048 * (it % 2) + 1024]
        sb = H[:, 2048 * (it % 2) + 1024:2048 * (it % 2) + 2048]
        if it >= 2:
            qwait(GPSr, "vva", steps4[it - 2])
            qwait(SYNr, "vva", steps4[it - 2])
        xld.append(dma(GPSr, "dl0", xb, xo_tm[128 * tc:128 * (tc + 1), 1024 * hf:1024 * (hf + 1)]))
        sld.append(dma(SYNr, "dl1", sb, s_dram[tc][:, 1024 * hf:1024 * (hf + 1)]))
        V.wait("dl0", xld[-1]); V.wait("dl1", sld[-1])
        Gc = G[:, 1024 * it:1024 * (it + 1)]
        V.scalar_tensor_tensor(Gc, xb, stats[:, tc:tc + 1], Gc, ALU.mult, ALU.add)
        S.wait("vva")
        S.activation(Gc, Gc, AF.Silu)
        V.wait("ssa")
        V.tensor_tensor(Gc, Gc, sb, ALU.mult)
        steps4.append(V.tensor_tensor(Gc, Gc, xb, ALU.add))
        qwait(SCAr, "vva", steps4[-1])
        dma(SCAr, "ds0" if it % 2 == 0 else "ds1", oo[tc][:, 1024 * hf:1024 * (hf + 1)], Gc)

    return finish()


def kernel(**inputs):
    nc = build_program()
    in_maps = host_prep(inputs)
    res = run_bass_kernel_spmd(nc, in_maps, core_ids=list(range(8)))
    out = np.empty((B, T, E), dtype=np.float32)
    for c in range(8):
        b, p = c // 2, c % 2
        out[b, own_rows(p)] = res.results[c]["oo"].reshape(R, E)
    return out
